# version 15
# baseline (speedup 1.0000x reference)
import math
from contextlib import ExitStack

import numpy as np
import ml_dtypes

import concourse.bass as bass
import concourse.mybir as mybir
from concourse.bass_utils import run_bass_kernel_spmd

F32 = mybir.dt.float32
BF16 = mybir.dt.bfloat16
AF = mybir.ActivationFunctionType
ALU = mybir.AluOpType
AX = mybir.AxisListType

BF = ml_dtypes.bfloat16

S = 4096
D = 4096
HD = 128
NKV = 8
NH = 32
TW = 512
NTB = S // TW
NCD = D // 128
TRIPS = (32, 24, 16, 8)
NSLOT = 4
NQ = NSLOT * TW
KVD = NKV * HD
EPS = 1e-5
ISQ = 1.0 / math.sqrt(HD)
J_SETS = ((7, 4, 3, 0), (6, 5, 2, 1))
CONST_VALS = (EPS, ISQ, -0.5, 0.5, 2.0, -ISQ, 1.0 / D, 8.0, -1.0)

ENGINES = ("sync", "gpsimd", "vector", "scalar", "tensor")
ENG_SEM = {"tensor": "s_pe", "vector": "s_dve", "scalar": "s_act",
           "gpsimd": "s_pool"}
COMPUTE_SEMS = {"s_pe", "s_dve", "s_act", "s_pool"}
DMA_LANES = {"sync": tuple(f"sl{i}" for i in range(6)),
             "gpsimd": tuple(f"gl{i}" for i in range(4))}
ALL_SEMS = tuple(ENG_SEM.values()) + DMA_LANES["sync"] + DMA_LANES["gpsimd"]


class Sched:

    def __init__(self):
        self.ev = {e: [] for e in ENGINES}
        self.cnt = {}
        self.writes = {}
        self.reads = {}
        self.lane_idx = {"sync": 0, "gpsimd": 0}

    def c(self, sem):
        return self.cnt.get(sem, 0)

    def _conf(self, name):
        pref = name + "."
        for k in set(self.writes) | set(self.reads):
            if k == name or k.startswith(pref) or name.startswith(k + "."):
                yield k

    def add(self, eng, fn, reads=(), writes=(), extra_waits=(), dma_sem=None,
            no_inc=False):
        lane_wait = None
        if dma_sem is not None:
            lanes = DMA_LANES[eng]
            lane = lanes[self.lane_idx[eng] % len(lanes)]
            self.lane_idx[eng] += 1
            sem, amt = lane, 16
            lane_wait = (lane, self.c(lane))
        elif no_inc:
            sem, amt = None, 0
        else:
            sem, amt = ENG_SEM[eng], 1
        waits = {}

        def need(s, v):
            if v <= 0:
                return
            if s == "s_pe" and eng == "tensor":
                return
            if waits.get(s, 0) < v:
                waits[s] = v

        for b in reads:
            for k in self._conf(b):
                for (s, v) in self.writes.get(k, ()):
                    need(s, v)
        for b in writes:
            for k in self._conf(b):
                for (s, v) in self.writes.get(k, ()):
                    need(s, v)
                for (s, v) in self.reads.get(k, ()):
                    need(s, v)
        for (s, v) in extra_waits:
            need(s, v)
        if lane_wait is not None:
            need(*lane_wait)

        if no_inc:
            self.ev[eng].append((sorted(waits.items()), fn, None, 0))
            return None
        post = self.cnt[sem] = self.c(sem) + amt
        self.ev[eng].append((sorted(waits.items()), fn, sem, amt))

        for b in writes:
            for k in list(self._conf(b)):
                if k != b:
                    self.writes.pop(k, None)
                    self.reads.pop(k, None)
            self.writes[b] = [(sem, post)]
            self.reads[b] = []
        for b in reads:
            rl = [(s, v) for (s, v) in self.reads.get(b, ()) if s != sem]
            rl.append((sem, post))
            self.reads[b] = rl
        return (sem, post)

    def mark_write(self, name, sem):
        for k in list(self._conf(name)):
            if k != name:
                self.writes.pop(k, None)
                self.reads.pop(k, None)
        self.writes[name] = [(sem, self.c(sem))]
        self.reads[name] = []

    def emit(self, block, sems, preamble=()):
        def mk(name):
            def run(eng):
                waited = {}
                if name != "sync":
                    for (s, v) in preamble:
                        eng.wait_ge(sems[s], v)
                        waited[s] = v
                for w, fn, sem, amt in self.ev[name]:
                    for (s, v) in w:
                        if waited.get(s, 0) < v:
                            eng.wait_ge(sems[s], v)
                            waited[s] = v
                    ins = fn(eng)
                    if sem is not None:
                        assert ins is not None, f"event on {name} emitted nothing"
                        ins.then_inc(sems[sem], amt)
            return run

        block.sync(mk("sync"))
        block.gpsimd(mk("gpsimd"))
        block.vector(mk("vector"))
        block.scalar(mk("scalar"))
        block.tensor(mk("tensor"))


def build_program(dbg=False):
    nc = bass.Bass("TRN2", target_bir_lowering=False)

    def din(name, shape, dt=F32):
        return nc.dram_tensor(name, shape, dt, kind="ExternalInput").ap()

    xT = din("xT", [D, S])
    xqT = din("xqT", [D, NQ])
    cosk = din("cosk", [HD, S], BF16)
    sink = din("sink", [HD, S], BF16)
    cosq = din("cosq", [HD, NQ], BF16)
    sinq = din("sinq", [HD, NQ], BF16)
    Wq = din("Wq", [D, D])
    Wk = din("Wk", [D, KVD])
    Wv = din("Wv", [D, KVD])
    Wo = din("Wo", [D, D])
    Wr = din("Wr", [NKV, D])
    br = din("br", [1, NKV])
    Al = din("Al", [D, 128])
    Bml = din("Bml", [128, D])
    E8 = din("E8", [NKV, 128], BF16)
    masks = din("masks", [NSLOT, 8, 128, TW], BF16)
    ones_col = din("ones_col", [128, 1], BF16)
    ones_row = din("ones_row", [1, 128], F32)
    ones_row_bf = din("ones_row_bf", [1, 128], BF16)
    piT = din("piT", [128, 128], BF16)
    ident_bf = din("ident_bf", [128, 128], BF16)
    ident_f = din("ident_f", [128, 8], F32)
    constv = din("constv", [128, 12], F32)

    yqT = nc.dram_tensor("yqT", [D, NQ], F32, kind="ExternalOutput").ap()
    wqc = nc.dram_tensor("wqc", [NH, 128, NCD * 128], BF16, kind="Internal").ap()
    woc = nc.dram_tensor("woc", [NH, 128, NCD * 128], BF16, kind="Internal").ap()
    wkc = nc.dram_tensor("wkc", [NKV, 128, NCD * 128], BF16, kind="Internal").ap()
    wvc = nc.dram_tensor("wvc", [NKV, 128, NCD * 128], BF16, kind="Internal").ap()
    sk = "ExternalOutput" if dbg else "Internal"
    kfs = nc.dram_tensor("kfs", [NKV, 128, S], BF16, kind=sk).ap()
    vts = nc.dram_tensor("vts", [NKV, 128, S], BF16, kind=sk).ap()
    if dbg:
        drec = nc.dram_tensor("drec", [1, TW], BF16, kind="ExternalOutput").ap()
        drb = nc.dram_tensor("drb", [128, TW], BF16, kind="ExternalOutput").ap()
        dstats = nc.dram_tensor("dstats", [2 * NTB, TW], F32, kind="ExternalOutput").ap()
        dq = nc.dram_tensor("dq", [NH, 128, TW], BF16, kind="ExternalOutput").ap()
        dattn = nc.dram_tensor("dattn", [NH, 128, TW], BF16, kind="ExternalOutput").ap()
        dgate = nc.dram_tensor("dgate", [NCD, 128, TW], BF16, kind="ExternalOutput").ap()
        dkm = nc.dram_tensor("dkm", [1, NKV], F32, kind="ExternalOutput").ap()
        dnb = nc.dram_tensor("dnb", [NKV, 128], F32, kind="ExternalOutput").ap()

    sch = Sched()
    A = sch.add

    with ExitStack() as ctx:
        en = ctx.enter_context

        def sb(name, shape, dt):
            return en(nc.sbuf_tensor(name, shape, dt))

        XB0 = sb("XB0", [128, NCD * TW], BF16)
        XB1 = sb("XB1", [128, NCD * TW], BF16)

        WKV = [sb(f"wkv{i}", [128, S], BF16) for i in range(4)]
        wk_sb = [WKV[0], WKV[2]]
        wv_sb = [WKV[1], WKV[3]]
        wkn = ["wkv0", "wkv2"]
        wvn = ["wkv1", "wkv3"]
        ksb = [WKV[0], WKV[2]]
        vtb = [WKV[1], WKV[3]]
        ksn = ["wkv0", "wkv2"]
        vtn = ["wkv1", "wkv3"]

        wq_sb = [sb(f"wq{i}", [128, NCD * 128], BF16) for i in range(3)]
        wo_sb = [sb(f"wo{i}", [128, NCD * 128], BF16) for i in range(3)]

        cosb = [sb(f"cosb{i}", [128, TW], BF16) for i in range(2)]
        sinb = [sb(f"sinb{i}", [128, TW], BF16) for i in range(2)]

        tmpA = sb("tmpA", [128, TW], F32)
        tmpB = sb("tmpB", [128, TW], F32)

        qpb = [sb(f"qpb{i}", [128, 4 * TW], BF16) for i in range(2)]
        PP = [sb(f"PP{i}", [128, 2 * TW], BF16) for i in range(2)]
        kraw = [PP[0][:, 0:TW], PP[0][:, TW:2 * TW]]
        krawn = ["PP0.a", "PP0.b"]
        sqh = PP[1][:, 0:TW]
        kout = [qpb[0][:, 0:TW], qpb[0][:, TW:2 * TW]]
        koutn = ["qpb0.h0", "qpb0.h1"]
        vout = [qpb[0][:, 2 * TW:3 * TW], qpb[0][:, 3 * TW:4 * TW]]
        voutn = ["qpb0.h2", "qpb0.h3"]
        vtst = [qpb[1][:, 0:TW], qpb[1][:, TW:2 * TW]]
        vtstn = ["qpb1.h0", "qpb1.h1"]

        m_sb = sb("m_sb", [128, TW], BF16)
        r_sb = sb("r_sb", [128, TW], BF16)
        rb0, rb1 = m_sb, r_sb
        xf32 = [sb(f"xf32{i}", [128, TW], F32) for i in range(2)]
        tmpc = tmpA

        mask_sb = sb("mask_sb", [128, 8 * TW], BF16)
        low_sb = sb("low_sb", [128, TW], BF16)
        scb_sb = sb("scb_sb", [128, TW], BF16)
        stsq = [low_sb, scb_sb]
        stsqn = ["low_sb", "scb_sb"]
        lg_sb = tmpB[0:8, 0:TW]
        lgt_sb = sb("lgt_sb", [128, 32], F32)
        et_sb = sb("et_sb", [128, 32], F32)
        rmax4 = sb("rmax4", [128, 4], F32)
        esum4 = sb("esum4", [128, 4], F32)
        rcp4 = sb("rcp4", [128, 4], F32)
        sct_sb = sb("sct_sb", [128, 32], BF16)
        sfm_sb = tmpB[0:8, 0:256].bitcast(BF16)

        srow = sb("srow", [1, TW], F32)
        m_row = sb("m_row", [1, TW], F32)
        r_row = sb("r_row", [1, TW], BF16)
        drr = [sb(f"drr{i}", [1, TW], BF16) for i in range(4)]
        mrbf = drr[0]
        kmax2 = sb("kmax2", [1, NKV], F32)
        qn4 = sb("qn4", [1, 4], F32)
        sc1 = sb("sc1", [1, 1], F32)
        negB = [sb(f"negB{i}", [128, 1], F32) for i in range(2)]

        onec = sb("onec", [128, 1], BF16)
        oner = sb("oner", [1, 128], F32)
        onerb = sb("onerb", [1, 128], BF16)
        pib = sb("pib", [128, 128], BF16)
        idb = sb("idb", [128, 128], BF16)
        idf = sb("idf", [128, 8], F32)
        e8b = sb("e8b", [8, 128], BF16)
        brb = sb("brb", [1, NKV], F32)
        br_b = sb("br_b", [128, NKV], F32)
        wr_bf = XB0[0:8, 0:D]
        wrt = sb("wrt", [128, NCD * 8], BF16)
        a_sb = sb("a_sb", [128, NCD * 128], BF16)
        bm_sb = sb("bm_sb", [128, D], BF16)

        cvt = sb("cvt", [128, 12], F32)
        pbigA = en(nc.psum_tensor("pbigA", [128, 1024], F32))
        pbigB = en(nc.psum_tensor("pbigB", [128, 1024], F32))
        A0 = en(nc.psum_tensor("A0", [128, 512], F32))
        A1 = en(nc.psum_tensor("A1", [128, 512], F32))
        D0 = en(nc.psum_tensor("D0", [128, 512], F32))
        D1 = en(nc.psum_tensor("D1", [128, 512], F32))
        d0bf = D0[:].bitcast(BF16)
        a1bf = A1[:].bitcast(BF16)

        sems = {n: en(nc.semaphore(n)) for n in ALL_SEMS}

        blk = en(nc.Block())

        cvt_done = A("sync", lambda s: s.dma_start(out=cvt[:], in_=constv),
                     writes=["cvt"], dma_sem="s_ld")
        for i, val in enumerate(CONST_VALS):
            nc.const_aps.register(F32, val, cvt[:, i:i + 1])
        for dst, nm, src in ((onec, "onec", ones_col), (oner, "oner", ones_row),
                             (onerb, "onerb", ones_row_bf),
                             (pib, "pib", piT), (idb, "idb", ident_bf),
                             (idf, "idf", ident_f), (e8b, "e8b", E8),
                             (brb, "brb", br)):
            A("sync", lambda s, dst=dst, src=src: s.dma_start(out=dst[:], in_=src),
              writes=[nm], dma_sem="s_ld")
        A("gpsimd", lambda g: g.dma_start(
            out=a_sb[:].rearrange("p (c r) -> p c r", c=NCD),
            in_=Al.rearrange("(c p) r -> p c r", p=128)),
          writes=["a_sb"], dma_sem="s_cast")
        A("gpsimd", lambda g: g.dma_start(out=bm_sb[:], in_=Bml),
          writes=["bm_sb"], dma_sem="s_cast")

        A("tensor", lambda t: t.matmul(A0[:, 0:8], oner[:], brb[:], start=True, stop=True),
          reads=["oner", "brb"], writes=["A0"])
        A("vector", lambda v: v.tensor_copy(br_b[:], A0[:, 0:8]),
          reads=["A0"], writes=["br_b"])

        A("gpsimd", lambda g: g.dma_start(out=wr_bf, in_=Wr),
          writes=["XB0"], dma_sem="s_cast")
        for c in range(NCD):
            A("tensor", lambda t, c=c: t.transpose(
                d0bf[:, c * 8:(c + 1) * 8], wr_bf[:, c * 128:(c + 1) * 128],
                idb[0:8, 0:8]),
              reads=["XB0", "idb"], writes=["D0"] if c == 0 else [])
        sch.mark_write("D0", "s_pe")
        A("vector", lambda v: v.tensor_copy(wrt[:], d0bf[:, 0:NCD * 8]),
          reads=["D0"], writes=["wrt"])

        CNAME = {id(wqc): "wqc", id(woc): "woc", id(wkc): "wkc", id(wvc): "wvc"}

        def cache_jobs_list():
            stage = [(wq_sb[i], f"wq{i}") for i in range(3)] + \
                    [(wo_sb[i], f"wo{i}") for i in range(3)]
            jobs = ([(Wq, wqc, n) for n in range(NH)]
                    + [(Wo, woc, n) for n in range(NH)])
            out = []
            for i, (W, C, n) in enumerate(jobs):
                if i < 54:
                    buf, bname = stage[i % len(stage)]
                else:
                    buf, bname = stage[3 + (i - 53) % 3]

                def job(W=W, C=C, n=n, buf=buf, bname=bname):
                    A("gpsimd", lambda g: g.dma_start(
                        out=buf[:].rearrange("p (c n) -> p c n", c=NCD),
                        in_=W[:, n * 128:(n + 1) * 128].rearrange(
                            "(c p) n -> p c n", p=128)),
                      writes=[bname], dma_sem="s_cast")
                    A("sync", lambda s: s.dma_start(out=C[n], in_=buf[:]),
                      reads=[bname], writes=[f"{CNAME[id(C)]}.{n}"],
                      dma_sem="s_st")
                out.append(job)
            return out

        def load_xblock(buf, bufname, src, col0):
            return A("gpsimd", lambda g: g.dma_start(
                out=buf[:].rearrange("p (c t) -> p c t", c=NCD),
                in_=src[:, col0:col0 + TW].rearrange("(c p) t -> p c t", p=128)),
                writes=[bufname], dma_sem="s_cast")

        def load_wcol(W, n0, buf, bufname):
            return A("gpsimd", lambda g: g.dma_start(
                out=buf[:].rearrange("p (c n) -> p c n", c=NCD),
                in_=W[:, n0:n0 + 128].rearrange("(c p) n -> p c n", p=128)),
                writes=[bufname], dma_sem="s_cast")

        def load_wcached(C, n, buf, bufname, eng="sync"):
            return A(eng, lambda s: s.dma_start(out=buf[:], in_=C[n]),
                     reads=[f"{CNAME[id(C)]}.{n}"], writes=[bufname],
                     dma_sem="s_ld")

        def rep(ap2d, n, width):
            return ap2d.unsqueeze(1).to_broadcast([128, n, width])

        def recip_bf(v, out, in_):
            with nc.allow_low_precision(reason="reciprocal feeds bf16 broadcast"):
                return v.reciprocal(out, in_)

        def ln_stats(xbuf, xname, dbg_tb=None, load=None):
            for c in range(NCD + 2):
                if load is not None and c % 8 == 0 and c < NCD:
                    srcT, col0 = load
                    p = c // 8
                    A("gpsimd", lambda g, p=p, srcT=srcT, col0=col0: g.dma_start(
                        out=xbuf[:, p * 8 * TW:(p + 1) * 8 * TW].rearrange(
                            "p (c t) -> p c t", c=8),
                        in_=srcT[p * 8 * 128:(p + 1) * 8 * 128,
                                 col0:col0 + TW].rearrange(
                            "(c p) t -> p c t", p=128)),
                      writes=[f"{xname}.p{p}"]
                             + [f"{xname}.h{8 * p + i}" for i in range(8)]
                             + [f"{xname}.g{8 * p + i}" for i in range(8)],
                      dma_sem="s_cast")
                if c < NCD:
                    sq, sqn = stsq[c % 2], stsqn[c % 2]
                    A("scalar", lambda s, c=c, sq=sq: s.activation(
                        sq[:], xbuf[:, c * TW:(c + 1) * TW], AF.Square),
                      reads=[f"{xname}.p{c // 8}"], writes=[sqn])
                if c >= 2:
                    c2 = c - 2
                    sq2, sq2n = stsq[c2 % 2], stsqn[c2 % 2]
                    A("tensor", lambda t, c2=c2: t.matmul(
                        D0[0:1, :], onec[:], xbuf[:, c2 * TW:(c2 + 1) * TW],
                        start=(c2 == 0), stop=(c2 == NCD - 1)),
                      reads=[f"{xname}.p{c2 // 8}", "onec"],
                      writes=["D0"] if c2 == 0 else [])
                    A("tensor", lambda t, c2=c2, sq2=sq2: t.matmul(
                        D1[0:1, :], onec[:], sq2[:],
                        start=(c2 == 0), stop=(c2 == NCD - 1)),
                      reads=[sq2n, "onec"], writes=["D1"] if c2 == 0 else [])
            sch.mark_write("D0", "s_pe")
            sch.mark_write("D1", "s_pe")
            A("vector", lambda v: v.tensor_scalar_mul(m_row[:], D0[0:1, :], 1.0 / D),
              reads=["D0"], writes=["m_row"])
            A("vector", lambda v: v.tensor_scalar_mul(srow[:], D1[0:1, :], 1.0 / D),
              reads=["D1"], writes=["srow"])
            A("vector", lambda v: v.tensor_mul(r_row[:], m_row[:], m_row[:]),
              reads=["m_row"], writes=["r_row"])
            A("vector", lambda v: v.tensor_sub(srow[:], srow[:], r_row[:]),
              reads=["srow", "r_row"], writes=["srow"])
            A("scalar", lambda s: s.activation(srow[:], srow[:], AF.Ln, bias=EPS),
              reads=["srow"], writes=["srow"])
            A("scalar", lambda s: s.activation(r_row[:], srow[:], AF.Exp, scale=-0.5),
              reads=["srow"], writes=["r_row"])
            A("vector", lambda v: v.tensor_copy(mrbf[:], m_row[:]),
              reads=["m_row"], writes=["drr0"])
            A("tensor", lambda t: t.matmul(D0[:], onerb[:], mrbf[:], start=True, stop=True),
              reads=["drr0", "onerb"], writes=["D0"])
            A("vector", lambda v: v.tensor_copy(m_sb[:], D0[:]),
              reads=["D0"], writes=["m_sb"])
            A("tensor", lambda t: t.matmul(D1[:], onerb[:], r_row[:], start=True, stop=True),
              reads=["r_row", "onerb"], writes=["D1"])
            A("vector", lambda v: v.tensor_copy(r_sb[:], D1[:]),
              reads=["D1"], writes=["r_sb"])
            if dbg and dbg_tb is not None:
                A("sync", lambda s: s.dma_start(
                    out=dstats[2 * dbg_tb:2 * dbg_tb + 1, :], in_=m_row[:]),
                  reads=["m_row"], dma_sem="s_st")
                A("gpsimd", lambda s: s.dma_start(
                    out=dstats[2 * dbg_tb + 1:2 * dbg_tb + 2, :], in_=r_row[:]),
                  reads=["r_row"], dma_sem="s_st")
            x3 = xbuf[:].rearrange("p (c t) -> p c t", c=NCD)
            A("vector", lambda v: v.tensor_tensor(
                x3, x3, rep(m_sb[:], NCD, TW), ALU.subtract),
              reads=[xname, "m_sb"], writes=[xname])
            A("vector", lambda v: v.tensor_tensor(
                x3, x3, rep(r_sb[:], NCD, TW), ALU.mult),
              reads=[xname, "r_sb"], writes=[xname])

        def p1_load(tbn):
            buf, name = (XB0, "XB0") if tbn % 2 == 0 else (XB1, "XB1")
            np_ = tbn % 2
            load_xblock(buf, name, xT, tbn * TW)
            A("sync", lambda s, tbn=tbn, np_=np_: s.dma_start(
                out=cosb[np_][:], in_=cosk[:, tbn * TW:(tbn + 1) * TW]),
              writes=[f"cosb{np_}"], dma_sem="s_ld")
            A("sync", lambda s, tbn=tbn, np_=np_: s.dma_start(
                out=sinb[np_][:], in_=sink[:, tbn * TW:(tbn + 1) * TW]),
              writes=[f"sinb{np_}"], dma_sem="s_ld")

        def p1_stats(tbn):
            buf, name = (XB0, "XB0") if tbn % 2 == 0 else (XB1, "XB1")
            ln_stats(buf, name, dbg_tb=tbn)

        def p2_slot_setup(slot, load_kv=True):
            spar = slot % 2
            xnb_, xnn_ = (XB0, "XB0") if spar == 0 else (XB1, "XB1")
            q0 = slot * TW
            T = TRIPS[slot]
            A("sync", lambda s, q0=q0, spar=spar: s.dma_start(
                out=cosb[spar][:], in_=cosq[:, q0:q0 + TW]),
              writes=[f"cosb{spar}"], dma_sem="s_ld")
            A("sync", lambda s, q0=q0, spar=spar: s.dma_start(
                out=sinb[spar][:], in_=sinq[:, q0:q0 + TW]),
              writes=[f"sinb{spar}"], dma_sem="s_ld")
            A("sync", lambda s, slot=slot: s.dma_start(
                out=mask_sb[:].rearrange("p (j t) -> p j t", j=8),
                in_=masks[slot].rearrange("j p t -> p j t")),
              writes=["mask_sb"], dma_sem="s_ld")
            load_xblock(xnb_, xnn_, xqT, q0)
            ln_stats(xnb_, xnn_)
            if load_kv:
                p2_kv_load(slot)

        def p2_kv_load(slot):
            T = TRIPS[slot]
            A("sync", lambda s, T=T: s.dma_start(
                out=ksb[0][:, 0:T * 128], in_=kfs[0][:, 0:T * 128]),
              reads=["kfs.g0"], writes=[ksn[0]], dma_sem="s_ld")
            A("sync", lambda s, T=T: s.dma_start(
                out=vtb[0][:, 0:T * 128], in_=vts[0][:, 0:T * 128]),
              reads=["vts.g0"], writes=[vtn[0]], dma_sem="s_ld")

        cache_jobs = cache_jobs_list()
        pending_vt = []
        p1_load(0)
        p1_stats(0)
        for g2 in range(2):
            load_wcol(Wk, g2 * 128, wk_sb[g2], wkn[g2])
            load_wcol(Wv, g2 * 128, wv_sb[g2], wvn[g2])

        for tb in range(NTB):
            par = tb % 2
            xbuf, xname = (XB0, "XB0") if par == 0 else (XB1, "XB1")
            t0 = tb * TW

            for g in range(NKV):
                if pending_vt:
                    pending_vt.pop(0)()
                if g == 0 and tb + 1 < NTB:
                    p1_load(tb + 1)
                if g == 5 and tb + 1 < NTB:
                    p1_stats(tb + 1)
                if g == 2 and tb + 1 == NTB:
                    p2_slot_setup(0, load_kv=False)
                    for h3 in range(3):
                        load_wcol(Wq, h3 * 128, wq_sb[h3], f"wq{h3}")
                gp = g % 2
                PK = pbigA if gp == 0 else pbigB
                PKn = "pbigA" if gp == 0 else "pbigB"
                PV_ = pbigB if gp == 0 else pbigA
                PVn = "pbigB" if gp == 0 else "pbigA"
                for c in range(NCD):
                    A("tensor", lambda t, c=c, gp=gp, xbuf=xbuf, PK=PK: t.matmul(
                        PK[:, 0:512], wk_sb[gp][:, c * 128:(c + 1) * 128],
                        xbuf[:, c * TW:(c + 1) * TW],
                        start=(c == 0), stop=(c == NCD - 1)),
                      reads=[wkn[gp], xname], writes=[PKn + ".lo"] if c == 0 else [])
                sch.mark_write(PKn + ".lo", "s_pe")
                A("scalar", lambda s, gp=gp, PK=PK: s.activation(
                    kraw[gp], PK[:, 0:512], AF.Copy),
                  reads=[PKn + ".lo"], writes=[krawn[gp]])
                A("tensor", lambda t, gp=gp, PK=PK: t.matmul(
                    PK[:, 512:1024], pib[:], kraw[gp], start=True, stop=True),
                  reads=[krawn[gp], "pib"], writes=[PKn + ".hi"])
                A("vector", lambda v, gp=gp, par=par: v.tensor_mul(
                    tmpA[:], kraw[gp], cosb[par][:]),
                  reads=[krawn[gp], f"cosb{par}"], writes=["tmpA"])
                A("vector", lambda v, par=par, PK=PK: v.tensor_mul(
                    tmpB[:], PK[:, 512:1024], sinb[par][:]),
                  reads=[PKn + ".hi", f"sinb{par}"], writes=["tmpB"])
                A("vector", lambda v, gp=gp: v.tensor_add(kout[gp], tmpA[:], tmpB[:]),
                  reads=["tmpA", "tmpB"], writes=[koutn[gp]])
                A("gpsimd", lambda gg, gp=gp: gg.tensor_mul(sqh, kout[gp], kout[gp]),
                  reads=[koutn[gp]], writes=["PP1.a"])
                A("tensor", lambda t: t.matmul(A0[0:1, :], onec[:], sqh,
                                               start=True, stop=True),
                  reads=["PP1.a", "onec"], writes=["A0"])
                if tb == 0:
                    A("vector", lambda v, g=g: v.reduce_max(
                        kmax2[0:1, g:g + 1], A0[0:1, :], axis=AX.X),
                      reads=["A0"], writes=["kmax2"])
                else:
                    A("vector", lambda v: v.reduce_max(sc1[:], A0[0:1, :], axis=AX.X),
                      reads=["A0"], writes=["sc1"])
                    A("vector", lambda v, g=g: v.tensor_max(
                        kmax2[0:1, g:g + 1], kmax2[0:1, g:g + 1], sc1[:]),
                      reads=["kmax2", "sc1"], writes=["kmax2"])
                A("sync", lambda s, g=g, t0=t0, gp=gp: s.dma_start(
                    out=kfs[g][:, t0:t0 + TW], in_=kout[gp]),
                  reads=[koutn[gp]], writes=[f"kfs.g{g}"], dma_sem="s_st")

                for c in range(NCD):
                    A("tensor", lambda t, c=c, gp=gp, xbuf=xbuf, PV_=PV_: t.matmul(
                        PV_[:, 0:512], wv_sb[gp][:, c * 128:(c + 1) * 128],
                        xbuf[:, c * TW:(c + 1) * TW],
                        start=(c == 0), stop=(c == NCD - 1)),
                      reads=[wvn[gp], xname], writes=[PVn + ".lo"] if c == 0 else [])
                sch.mark_write(PVn + ".lo", "s_pe")
                A("scalar", lambda s, gp=gp, PV_=PV_: s.activation(
                    vout[gp], PV_[:, 0:512], AF.Copy),
                  reads=[PVn + ".lo"], writes=[voutn[gp]])
                ng = g + 2
                if ng < NKV or tb + 1 < NTB:
                    ng = ng % NKV
                    load_wcol(Wk, ng * 128, wk_sb[gp], wkn[gp])
                    load_wcol(Wv, ng * 128, wv_sb[gp], wvn[gp])
                def _vtr(g=g, t0=t0, gp=gp):
                    for j in range(4):
                        A("tensor", lambda t, j=j, gp=gp: t.transpose(
                            a1bf[:, j * 128:(j + 1) * 128],
                            vout[gp][:, j * 128:(j + 1) * 128], idb[:]),
                          reads=[voutn[gp], "idb"], writes=["A1"] if j == 0 else [])
                    sch.mark_write("A1", "s_pe")
                    A("scalar", lambda s, gp=gp: s.activation(
                        vtst[gp], a1bf[:, 0:512], AF.Copy),
                      reads=["A1"], writes=[vtstn[gp]])
                    A("sync", lambda s, g=g, t0=t0, gp=gp: s.dma_start(
                        out=vts[g][:, t0:t0 + TW], in_=vtst[gp]),
                      reads=[vtstn[gp]], writes=[f"vts.g{g}"], dma_sem="s_st")
                pending_vt.append(_vtr)
                if cache_jobs:
                    cache_jobs.pop(0)()

        assert not cache_jobs
        for fn_ in pending_vt:
            fn_()
        pending_vt.clear()
        if dbg:
            A("sync", lambda s: s.dma_start(out=dkm, in_=kmax2[:]),
              reads=["kmax2"], dma_sem="s_st")
        p2_kv_load(0)

        for slot in range(NSLOT):
            T = TRIPS[slot]
            spar = slot % 2
            xnb, xnname = (XB0, "XB0") if spar == 0 else (XB1, "XB1")
            aob, aoname = (XB1, "XB1") if spar == 0 else (XB0, "XB0")
            q0 = slot * TW

            pending_norm = []

            for g in range(NKV):
                gp = g % 2
                qbuf, qbname = qpb[gp], f"qpb{gp}"
                nbB, nbname = negB[gp], f"negB{gp}"

                projb = ((A0, "A0"), (A1, "A1"), (A0, "A0"), (A1, "A1"))
                rotb = ((D0, "D0"), (D1, "D1"), (D0, "D0"), (D1, "D1"))
                nrmb = ((pbigA[0:1, 0:512], "pbigA.lo"),
                        (pbigA[0:1, 512:1024], "pbigA.hi"),
                        (pbigB[0:1, 0:512], "pbigB.lo"),
                        (pbigB[0:1, 512:1024], "pbigB.hi"))
                krs = (PP[0][:, 0:TW], PP[0][:, TW:2 * TW],
                       PP[1][:, 0:TW], PP[1][:, TW:2 * TW])
                krn = ("PP0.a", "PP0.b", "PP1.a", "PP1.b")

                def proj(hl, g=g, xnb=xnb, xnname=xnname, projb=projb,
                         krs=krs, krn=krn, slot=slot):
                    h = 4 * g + hl
                    QB, QBn = projb[hl]
                    wqi = (slot * NH + h) % 3
                    for c in range(NCD):
                        A("tensor", lambda t, c=c, QB=QB, xnb=xnb, wqi=wqi: t.matmul(
                            QB[:], wq_sb[wqi][:, c * 128:(c + 1) * 128],
                            xnb[:, c * TW:(c + 1) * TW],
                            start=(c == 0), stop=(c == NCD - 1)),
                          reads=[f"wq{wqi}", xnname], writes=[QBn] if c == 0 else [])
                    sch.mark_write(QBn, "s_pe")
                    A("vector", lambda v, hl=hl, QB=QB, krs=krs: v.tensor_copy(
                        krs[hl], QB[:]),
                      reads=[QBn], writes=[krn[hl]])
                    nh = h + 3
                    if nh < NH or slot + 1 < NSLOT:
                        wqi2 = (slot * NH + nh) % 3
                        load_wcached(wqc, nh % NH, wq_sb[wqi2], f"wq{wqi2}")

                def rot(hl, rotb=rotb, krs=krs, krn=krn, qbuf=qbuf,
                        qbname=qbname, spar=spar):
                    RB, RBn = rotb[hl]
                    A("tensor", lambda t, RB=RB, krs=krs, hl=hl: t.matmul(
                        RB[:], pib[:], krs[hl], start=True, stop=True),
                      reads=[krn[hl], "pib"], writes=[RBn])
                    A("vector", lambda v, hl=hl, krs=krs, spar=spar: v.tensor_mul(
                        tmpA[:], krs[hl], cosb[spar][:]),
                      reads=[krn[hl], f"cosb{spar}"], writes=["tmpA"])
                    A("vector", lambda v, RB=RB, spar=spar: v.tensor_mul(
                        tmpB[:], RB[:], sinb[spar][:]),
                      reads=[RBn, f"sinb{spar}"], writes=["tmpB"])
                    A("vector", lambda v, hl=hl, qbuf=qbuf: v.tensor_add(
                        qbuf[:, hl * TW:(hl + 1) * TW], tmpA[:], tmpB[:]),
                      reads=["tmpA", "tmpB"], writes=[f"{qbname}.h{hl}"])
                    A("gpsimd", lambda gg, hl=hl, qbuf=qbuf, krs=krs: gg.tensor_mul(
                        krs[hl], qbuf[:, hl * TW:(hl + 1) * TW],
                        qbuf[:, hl * TW:(hl + 1) * TW]),
                      reads=[f"{qbname}.h{hl}"], writes=[krn[hl]])

                def nrm(hl, nrmb=nrmb, krs=krs, krn=krn):
                    NB, NBn = nrmb[hl]
                    A("tensor", lambda t, NB=NB, hl=hl, krs=krs: t.matmul(
                        NB, onec[:], krs[hl], start=True, stop=True),
                      reads=[krn[hl], "onec"], writes=[NBn])
                    A("vector", lambda v, hl=hl, NB=NB: v.reduce_max(
                        qn4[0:1, hl:hl + 1], NB, axis=AX.X),
                      reads=[NBn], writes=["qn4"])

                proj(0); proj(1)
                for fn_ in pending_norm:
                    fn_()
                pending_norm.clear()
                rot(0); proj(2); rot(1); proj(3)
                rot(2); rot(3)
                nrm(0); nrm(1); nrm(2); nrm(3)
                A("vector", lambda v: v.reduce_max(sc1[:], qn4[:], axis=AX.X),
                  reads=["qn4"], writes=["sc1"])
                A("vector", lambda v, g=g: v.tensor_mul(
                    sc1[:], sc1[:], kmax2[0:1, g:g + 1]),
                  reads=["sc1", "kmax2"], writes=["sc1"])
                A("scalar", lambda s: s.activation(sc1[:], sc1[:], AF.Ln),
                  reads=["sc1"], writes=["sc1"])
                A("scalar", lambda s: s.activation(sc1[:], sc1[:], AF.Exp, scale=0.5),
                  reads=["sc1"], writes=["sc1"])
                A("vector", lambda v: v.tensor_scalar_mul(sc1[:], sc1[:], -ISQ),
                  reads=["sc1"], writes=["sc1"])

                def _negb(nbB=nbB, nbname=nbname):
                    A("tensor", lambda t: t.matmul(A0[:, 0:1], oner[:], sc1[:],
                                                   start=True, stop=True),
                      reads=["sc1", "oner"], writes=["A0"])
                    A("vector", lambda v, nbB=nbB: v.tensor_copy(nbB[:], A0[:, 0:1]),
                      reads=["A0"], writes=[nbname])
                negb_fn = _negb
                if dbg and slot == 0:
                    for hl in range(4):
                        A("sync", lambda s, g=g, hl=hl, qbuf=qbuf: s.dma_start(
                            out=dq[4 * g + hl], in_=qbuf[:, hl * TW:(hl + 1) * TW]),
                          reads=[f"{qbname}.h{hl}"], dma_sem="s_st")


                if g + 1 < NKV:
                    ngp = (g + 1) % 2
                    A("sync", lambda s, g=g, T=T, ngp=ngp: s.dma_start(
                        out=ksb[ngp][:, 0:T * 128], in_=kfs[g + 1][:, 0:T * 128]),
                      reads=[f"kfs.g{g + 1}"], writes=[ksn[ngp]], dma_sem="s_ld")
                    A("sync", lambda s, g=g, T=T, ngp=ngp: s.dma_start(
                        out=vtb[ngp][:, 0:T * 128], in_=vts[g + 1][:, 0:T * 128]),
                      reads=[f"vts.g{g + 1}"], writes=[vtn[ngp]], dma_sem="s_ld")

                for eh in range(2):
                    ha = 4 * g + 2 * eh
                    la, lb = 2 * eh, 2 * eh + 1

                    def s_mm(kt, gp=gp, qbuf=qbuf, qbname=qbname, la=la,
                             lb=lb, T=T):
                        kp = kt % 2
                        PB = pbigA if kp == 0 else pbigB
                        PBn = "pbigA" if kp == 0 else "pbigB"
                        msk = kt >= T - 8
                        j = kt - (T - 8)
                        A("tensor", lambda t, kt=kt, PB=PB, gp=gp, qbuf=qbuf, la=la,
                          msk=msk: t.matmul(
                            PB[:, 0:512], ksb[gp][:, kt * 128:(kt + 1) * 128],
                            qbuf[:, la * TW:(la + 1) * TW], start=True, stop=not msk),
                          reads=[ksn[gp], f"{qbname}.h{la}"], writes=[PBn + ".lo"])
                        if msk:
                            A("tensor", lambda t, j=j, PB=PB: t.matmul(
                                PB[:, 0:512], idb[:],
                                mask_sb[:, j * TW:(j + 1) * TW],
                                start=False, stop=True),
                              reads=["idb", "mask_sb"], writes=[])
                            sch.mark_write(PBn + ".lo", "s_pe")
                        A("tensor", lambda t, kt=kt, PB=PB, gp=gp, qbuf=qbuf, lb=lb,
                          msk=msk: t.matmul(
                            PB[:, 512:1024], ksb[gp][:, kt * 128:(kt + 1) * 128],
                            qbuf[:, lb * TW:(lb + 1) * TW], start=True, stop=not msk),
                          reads=[ksn[gp], f"{qbname}.h{lb}"], writes=[PBn + ".hi"])
                        if msk:
                            A("tensor", lambda t, j=j, PB=PB: t.matmul(
                                PB[:, 512:1024], idb[:],
                                mask_sb[:, j * TW:(j + 1) * TW],
                                start=False, stop=True),
                              reads=["idb", "mask_sb"], writes=[])
                            sch.mark_write(PBn + ".hi", "s_pe")

                    s_mm(0)
                    if eh == 0 and negb_fn is not None:
                        negb_fn()
                        negb_fn = None
                    for kt in range(T):
                        kp = kt % 2
                        PB = pbigA if kp == 0 else pbigB
                        PBn = "pbigA" if kp == 0 else "pbigB"
                        Pb, Pbn = PP[kp], f"PP{kp}"
                        if kt + 1 < T:
                            s_mm(kt + 1)
                        A("scalar", lambda s, PB=PB, Pb=Pb, nbB=nbB: s.activation(
                            Pb[:], PB[:], AF.Exp, bias=nbB[:, 0:1], scale=ISQ),
                          reads=[PBn + ".lo", PBn + ".hi", nbname], writes=[Pbn])
                        A("tensor", lambda t, kt=kt, Pb=Pb, gp=gp, T=T: t.matmul(
                            A0[:], vtb[gp][:, kt * 128:(kt + 1) * 128], Pb[:, 0:512],
                            start=(kt == 0), stop=(kt == T - 1)),
                          reads=[vtn[gp], Pbn + ".lo"], writes=["A0"] if kt == 0 else [])
                        A("tensor", lambda t, kt=kt, Pb=Pb, T=T: t.matmul(
                            D0[0:1, :], onec[:], Pb[:, 0:512],
                            start=(kt == 0), stop=(kt == T - 1)),
                          reads=["onec", Pbn + ".lo"], writes=["D0"] if kt == 0 else [])
                        A("tensor", lambda t, kt=kt, Pb=Pb, gp=gp, T=T: t.matmul(
                            A1[:], vtb[gp][:, kt * 128:(kt + 1) * 128], Pb[:, 512:1024],
                            start=(kt == 0), stop=(kt == T - 1)),
                          reads=[vtn[gp], Pbn + ".hi"], writes=["A1"] if kt == 0 else [])
                        A("tensor", lambda t, kt=kt, Pb=Pb, T=T: t.matmul(
                            D1[0:1, :], onec[:], Pb[:, 512:1024],
                            start=(kt == 0), stop=(kt == T - 1)),
                          reads=["onec", Pbn + ".hi"], writes=["D1"] if kt == 0 else [])
                    for nmx in ("A0", "A1", "D0", "D1"):
                        sch.mark_write(nmx, "s_pe")
                    A("vector", lambda v, ha=ha, aob=aob: v.tensor_copy(
                        aob[:, ha * TW:(ha + 1) * TW], A0[:]),
                      reads=["A0"], writes=[f"{aoname}.h{ha}"])
                    A("vector", lambda v, ha=ha, aob=aob: v.tensor_copy(
                        aob[:, (ha + 1) * TW:(ha + 2) * TW], A1[:]),
                      reads=["A1"], writes=[f"{aoname}.h{ha + 1}"])
                    A("vector", lambda v, eh=eh: v.tensor_copy(
                        drr[2 * eh][:], D0[0:1, :]),
                      reads=["D0"], writes=[f"drr{2 * eh}"])
                    A("vector", lambda v, eh=eh: v.tensor_copy(
                        drr[2 * eh + 1][:], D1[0:1, :]),
                      reads=["D1"], writes=[f"drr{2 * eh + 1}"])

                    def _norm(ha=ha, eh=eh, aob=aob, aoname=aoname, g=g,
                              slot=slot):
                        for i2 in range(2):
                            dri, drn = drr[2 * eh + i2], f"drr{2 * eh + i2}"
                            half = slice(0, 512) if i2 == 0 else slice(512, 1024)
                            hname = f"{aoname}.h{ha + i2}"
                            A("scalar", lambda s, dri=dri: s.activation(
                                srow[:], dri[:], AF.Ln),
                              reads=[drn], writes=["srow"])
                            A("scalar", lambda s, dri=dri: s.activation(
                                dri[:], srow[:], AF.Exp, scale=-1.0),
                              reads=["srow"], writes=[drn])
                            A("tensor", lambda t, dri=dri, half=half: t.matmul(
                                pbigA[:, half], onerb[:], dri[:],
                                start=True, stop=True),
                              reads=[drn, "onerb"],
                              writes=["pbigA.lo" if half.start == 0 else "pbigA.hi"])
                            rb = rb0 if i2 == 0 else rb1
                            rbn = "m_sb" if i2 == 0 else "r_sb"
                            A("vector", lambda v, rb=rb, half=half: v.tensor_copy(
                                rb[:], pbigA[:, half]),
                              reads=["pbigA.lo" if half.start == 0 else "pbigA.hi"],
                              writes=[rbn])
                            A("vector", lambda v, ha=ha, i2=i2, aob=aob, rb=rb:
                              v.tensor_tensor(
                                aob[:, (ha + i2) * TW:(ha + i2 + 1) * TW],
                                aob[:, (ha + i2) * TW:(ha + i2 + 1) * TW],
                                rb[:], ALU.mult),
                              reads=[hname, rbn], writes=[hname])
                            if dbg and slot == 0:
                                A("sync", lambda s, ha=ha, i2=i2, aob=aob:
                                  s.dma_start(
                                    out=dattn[ha + i2],
                                    in_=aob[:, (ha + i2) * TW:(ha + i2 + 1) * TW]),
                                  reads=[hname], dma_sem="s_st")
                    pending_norm.append(_norm)

            for fn_ in pending_norm:
                fn_()
            pending_norm.clear()
            for c in range(NCD):
                A("tensor", lambda t, c=c, xnb=xnb: t.matmul(
                    A0[0:8, :], wrt[:, c * 8:(c + 1) * 8],
                    xnb[:, c * TW:(c + 1) * TW],
                    start=(c == 0), stop=(c == NCD - 1)),
                  reads=["wrt", xnname], writes=["A0"] if c == 0 else [])
            sch.mark_write("A0", "s_pe")
            A("vector", lambda v: v.tensor_copy(lg_sb, A0[0:8, :]),
              reads=["A0"], writes=["tmpB"])
            for j in range(4):
                A("tensor", lambda t, j=j: t.transpose(
                    A1[:, j * 8:(j + 1) * 8], lg_sb[:, j * 128:(j + 1) * 128],
                    idf[0:8, 0:8]),
                  reads=["tmpB", "idf"], writes=["A1"] if j == 0 else [])
            sch.mark_write("A1", "s_pe")
            A("vector", lambda v: v.tensor_copy(lgt_sb[:], A1[:, 0:32]),
              reads=["A1"], writes=["lgt_sb"])
            A("vector", lambda v: v.tensor_tensor(
                lgt_sb[:].rearrange("p (j h) -> p j h", j=4),
                lgt_sb[:].rearrange("p (j h) -> p j h", j=4),
                br_b[:].unsqueeze(1).to_broadcast([128, 4, NKV]), ALU.add),
              reads=["lgt_sb", "br_b"], writes=["lgt_sb"])
            A("vector", lambda v: v.reduce_max(
                rmax4[:], lgt_sb[:].rearrange("p (j h) -> p j h", j=4),
                axis=AX.X, negate=True),
              reads=["lgt_sb"], writes=["rmax4"])
            for j in range(4):
                A("scalar", lambda s, j=j: s.activation(
                    et_sb[:, j * 8:(j + 1) * 8], lgt_sb[:, j * 8:(j + 1) * 8],
                    AF.Exp, bias=rmax4[:, j:j + 1],
                    accum_out=esum4[:, j:j + 1]),
                  reads=["lgt_sb", "rmax4"], writes=["et_sb", "esum4"])
            A("vector", lambda v: v.reciprocal(rcp4[:], esum4[:]),
              reads=["esum4"], writes=["rcp4"])
            for j in range(4):
                A("vector", lambda v, j=j: v.tensor_scalar(
                    sct_sb[:, j * 8:(j + 1) * 8], et_sb[:, j * 8:(j + 1) * 8],
                    rcp4[:, j:j + 1], float(NKV), ALU.mult, ALU.mult),
                  reads=["et_sb", "rcp4"], writes=["sct_sb"])
            for j in range(4):
                A("tensor", lambda t, j=j: t.transpose(
                    d0bf[0:8, j * 128:(j + 1) * 128], sct_sb[:, j * 8:(j + 1) * 8],
                    idb[:]),
                  reads=["sct_sb", "idb"], writes=["D0"] if j == 0 else [])
            sch.mark_write("D0", "s_pe")
            A("vector", lambda v: v.tensor_copy(sfm_sb, d0bf[0:8, 0:TW]),
              reads=["D0"], writes=["tmpB"])
            A("tensor", lambda t: t.matmul(A0[:], e8b[:], sfm_sb,
                                           start=True, stop=True),
              reads=["e8b", "tmpB"], writes=["A0"])
            A("vector", lambda v: v.tensor_copy(scb_sb[:], A0[:]),
              reads=["A0"], writes=["scb_sb"])
            for c in range(NCD):
                A("tensor", lambda t, c=c, xnb=xnb: t.matmul(
                    A1[:], a_sb[:, c * 128:(c + 1) * 128],
                    xnb[:, c * TW:(c + 1) * TW],
                    start=(c == 0), stop=(c == NCD - 1)),
                  reads=["a_sb", xnname], writes=["A1"] if c == 0 else [])
            sch.mark_write("A1", "s_pe")
            A("vector", lambda v: v.tensor_tensor(low_sb[:], A1[:], scb_sb[:], ALU.mult),
              reads=["A1", "scb_sb"], writes=["low_sb"])
            for oc in range(NCD):
                PB = D0 if oc % 2 == 0 else D1
                PBn = "D0" if oc % 2 == 0 else "D1"
                A("tensor", lambda t, oc=oc, PB=PB: t.matmul(
                    PB[:], bm_sb[:, oc * 128:(oc + 1) * 128], low_sb[:],
                    start=True, stop=True),
                  reads=["bm_sb", "low_sb"], writes=[PBn])
                A("scalar", lambda s, oc=oc, PB=PB, xnb=xnb: s.activation(
                    xnb[:, oc * TW:(oc + 1) * TW], PB[:], AF.Silu, scale=2.0),
                  reads=[PBn], writes=[f"{xnname}.g{oc}"])
                if dbg and slot == 0:
                    A("sync", lambda s, oc=oc, xnb=xnb: s.dma_start(
                        out=dgate[oc], in_=xnb[:, oc * TW:(oc + 1) * TW]),
                      reads=[f"{xnname}.g{oc}"], dma_sem="s_st")

            for i3 in range(3):
                load_wcached(woc, i3, wo_sb[i3], f"wo{i3}", eng="gpsimd")
            A("sync", lambda s, q0=q0: s.dma_start(
                out=xf32[0][:], in_=xqT[0:128, q0:q0 + TW]),
              writes=["xf320"], dma_sem="s_ld")
            for nck in range(NCD):
                wp = nck % 2
                wb3 = nck % 3
                PB = pbigA if wp == 0 else pbigB
                PBn = "pbigA" if wp == 0 else "pbigB"
                for c in range(NCD):
                    A("tensor", lambda t, c=c, PB=PB, wb3=wb3, aob=aob: t.matmul(
                        PB[:, 0:512], wo_sb[wb3][:, c * 128:(c + 1) * 128],
                        aob[:, c * TW:(c + 1) * TW],
                        start=(c == 0), stop=(c == NCD - 1)),
                      reads=[f"wo{wb3}", f"{aoname}.h{c}"],
                      writes=[PBn + ".lo"] if c == 0 else [])
                sch.mark_write(PBn + ".lo", "s_pe")
                if nck + 3 < NCD:
                    load_wcached(woc, nck + 3, wo_sb[(nck + 3) % 3],
                                 f"wo{(nck + 3) % 3}", eng="gpsimd")
                if nck + 1 < NCD:
                    A("sync", lambda s, nck=nck, q0=q0: s.dma_start(
                        out=xf32[(nck + 1) % 2][:],
                        in_=xqT[(nck + 1) * 128:(nck + 2) * 128, q0:q0 + TW]),
                      writes=[f"xf32{(nck + 1) % 2}"], dma_sem="s_ld")
                A("vector", lambda v, nck=nck, PB=PB, xnb=xnb: v.tensor_tensor(
                    tmpc[:], PB[:, 0:512], xnb[:, nck * TW:(nck + 1) * TW], ALU.mult),
                  reads=[PBn + ".lo", f"{xnname}.g{nck}"], writes=["tmpA"])
                A("vector", lambda v, wp=wp: v.tensor_add(
                    xf32[wp][:], tmpc[:], xf32[wp][:]),
                  reads=["tmpA"], writes=[f"xf32{wp}"])
                A("sync", lambda s, nck=nck, q0=q0, wp=wp: s.dma_start(
                    out=yqT[nck * 128:(nck + 1) * 128, q0:q0 + TW], in_=xf32[wp][:]),
                  reads=[f"xf32{wp}"], dma_sem="s_st")
            if slot + 1 < NSLOT:
                p2_slot_setup(slot + 1)

        A("sync", lambda s: s.nop(),
          extra_waits=[(ln, sch.c(ln))
                       for ln in DMA_LANES["sync"] + DMA_LANES["gpsimd"]],
          no_inc=True)

        sch.emit(blk, sems, preamble=(cvt_done,))

    return nc


def _cos_sin_tables():
    inv = 1.0 / (10000.0 ** (np.arange(0, HD, 2, dtype=np.float64) / HD))
    pos = np.arange(S, dtype=np.float64)
    fr = pos[:, None] * inv[None, :]
    emb = np.concatenate([fr, fr], axis=1)
    return (np.ascontiguousarray(np.cos(emb).astype(np.float32).T),
            np.ascontiguousarray(np.sin(emb).astype(np.float32).T))


def make_in_maps(x, Wq_, Wk_, Wv_, Wo_, A_, B_, Wr_, br_):
    cosT, sinT = _cos_sin_tables()

    piT = np.zeros((128, 128), dtype=np.float32)
    for a in range(64):
        piT[64 + a, a] = -1.0
        piT[a, 64 + a] = 1.0
    E8m = np.zeros((NKV, 128), dtype=np.float32)
    for h in range(NKV):
        E8m[h, h * 16:(h + 1) * 16] = 1.0

    Al = np.ascontiguousarray(np.asarray(A_, np.float32).reshape(128, D).T)
    Bml = np.ascontiguousarray(
        np.asarray(B_, np.float32).transpose(0, 2, 1).reshape(128, D))

    shared = {
        "cosk": cosT.astype(BF), "sink": sinT.astype(BF),
        "Wq": np.ascontiguousarray(np.asarray(Wq_, np.float32)),
        "Wk": np.ascontiguousarray(np.asarray(Wk_, np.float32)),
        "Wv": np.ascontiguousarray(np.asarray(Wv_, np.float32)),
        "Wo": np.ascontiguousarray(np.asarray(Wo_, np.float32)),
        "Wr": np.ascontiguousarray(np.asarray(Wr_, np.float32)),
        "br": np.asarray(br_, np.float32).reshape(1, NKV),
        "Al": Al, "Bml": Bml,
        "E8": E8m.astype(BF),
        "ones_col": np.ones((128, 1), np.float32).astype(BF),
        "ones_row": np.ones((1, 128), np.float32),
        "ones_row_bf": np.ones((1, 128), np.float32).astype(BF),
        "piT": piT.astype(BF),
        "ident_bf": np.eye(128, dtype=np.float32).astype(BF),
        "ident_f": np.eye(128, dtype=np.float32)[:, 0:8].copy(),
        "constv": np.tile(np.pad(np.array(CONST_VALS, np.float32), (0, 12 - len(CONST_VALS))), (128, 1)),
    }

    in_maps = []
    for c in range(8):
        b, sub = c // 2, c % 2
        js = J_SETS[sub]
        xb = x[b]
        xTc = np.ascontiguousarray(xb.T)
        qcols = np.concatenate([np.arange(512 * J, 512 * (J + 1)) for J in js])
        xqTc = np.ascontiguousarray(xb[qcols].T)
        cosqc = np.ascontiguousarray(cosT[:, qcols])
        sinqc = np.ascontiguousarray(sinT[:, qcols])
        mk = np.zeros((NSLOT, 8, 128, TW), np.float32)
        for i, J in enumerate(js):
            T = TRIPS[i]
            for j in range(8):
                kt = T - 8 + j
                kglob = 128 * kt + np.arange(128)[:, None]
                qglob = 512 * J + np.arange(TW)[None, :]
                mk[i, j] = np.where(qglob >= kglob, 0.0, -30000.0)
        m = dict(shared)
        m.update({"xT": xTc, "xqT": xqTc, "cosq": cosqc.astype(BF),
                  "sinq": sinqc.astype(BF), "masks": mk.astype(BF)})
        in_maps.append(m)
    return in_maps


def kernel(L0_attn_hidden_states, L0_attn_Wq, L0_attn_Wk, L0_attn_Wv, L0_attn_Wo,
           L0_ffn_A, L0_ffn_B, L0_ffn_Wr, L0_ffn_br):
    x = np.asarray(L0_attn_hidden_states, dtype=np.float32)
    B = x.shape[0]
    in_maps = make_in_maps(x, L0_attn_Wq, L0_attn_Wk, L0_attn_Wv, L0_attn_Wo,
                           L0_ffn_A, L0_ffn_B, L0_ffn_Wr, L0_ffn_br)
    nc = build_program()
    res = run_bass_kernel_spmd(nc, in_maps, core_ids=list(range(8)))

    out = np.empty((B, S, D), dtype=np.float32)
    for c in range(8):
        b, sub = c // 2, c % 2
        js = J_SETS[sub]
        yq = res.results[c]["yqT"].T
        for i, J in enumerate(js):
            out[b, 512 * J:512 * (J + 1), :] = yq[i * TW:(i + 1) * TW, :]
    return out


# version 16 (final)
# speedup vs baseline: 1.1286x; 1.1286x over previous
import math
from contextlib import ExitStack

import numpy as np
import ml_dtypes

import concourse.bass as bass
import concourse.mybir as mybir
from concourse.bass_utils import run_bass_kernel_spmd

F32 = mybir.dt.float32
BF16 = mybir.dt.bfloat16
AF = mybir.ActivationFunctionType
ALU = mybir.AluOpType
AX = mybir.AxisListType

BF = ml_dtypes.bfloat16

S = 4096
D = 4096
HD = 128
NKV = 8
NH = 32
TW = 512
NTB = S // TW
NCD = D // 128
TRIPS = (32, 24, 16, 8)
NSLOT = 4
NQ = NSLOT * TW
KVD = NKV * HD
EPS = 1e-5
ISQ = 1.0 / math.sqrt(HD)
J_SETS = ((7, 4, 3, 0), (6, 5, 2, 1))
CONST_VALS = (EPS, ISQ, -0.5, 0.5, 2.0, -ISQ, 1.0 / D, 8.0, -1.0)

ENGINES = ("sync", "gpsimd", "vector", "scalar", "tensor")
ENG_SEM = {"tensor": "s_pe", "vector": "s_dve", "scalar": "s_act",
           "gpsimd": "s_pool"}
COMPUTE_SEMS = {"s_pe", "s_dve", "s_act", "s_pool"}
DMA_LANES = {"sync": tuple(f"sl{i}" for i in range(6)),
             "gpsimd": tuple(f"gl{i}" for i in range(4))}
ALL_SEMS = tuple(ENG_SEM.values()) + DMA_LANES["sync"] + DMA_LANES["gpsimd"]


class Sched:

    def __init__(self):
        self.ev = {e: [] for e in ENGINES}
        self.cnt = {}
        self.writes = {}
        self.reads = {}
        self.lane_idx = {"sync": 0, "gpsimd": 0}

    def c(self, sem):
        return self.cnt.get(sem, 0)

    def _conf(self, name):
        pref = name + "."
        for k in set(self.writes) | set(self.reads):
            if k == name or k.startswith(pref) or name.startswith(k + "."):
                yield k

    def add(self, eng, fn, reads=(), writes=(), extra_waits=(), dma_sem=None,
            no_inc=False):
        lane_wait = None
        if dma_sem is not None:
            lanes = DMA_LANES[eng]
            lane = lanes[self.lane_idx[eng] % len(lanes)]
            self.lane_idx[eng] += 1
            sem, amt = lane, 16
            lane_wait = (lane, self.c(lane))
        elif no_inc:
            sem, amt = None, 0
        else:
            sem, amt = ENG_SEM[eng], 1
        waits = {}

        def need(s, v):
            if v <= 0:
                return
            if s == "s_pe" and eng == "tensor":
                return
            if waits.get(s, 0) < v:
                waits[s] = v

        for b in reads:
            for k in self._conf(b):
                for (s, v) in self.writes.get(k, ()):
                    need(s, v)
        for b in writes:
            for k in self._conf(b):
                for (s, v) in self.writes.get(k, ()):
                    need(s, v)
                for (s, v) in self.reads.get(k, ()):
                    need(s, v)
        for (s, v) in extra_waits:
            need(s, v)
        if lane_wait is not None:
            need(*lane_wait)

        if no_inc:
            self.ev[eng].append((sorted(waits.items()), fn, None, 0))
            return None
        post = self.cnt[sem] = self.c(sem) + amt
        self.ev[eng].append((sorted(waits.items()), fn, sem, amt))

        for b in writes:
            for k in list(self._conf(b)):
                if k != b:
                    self.writes.pop(k, None)
                    self.reads.pop(k, None)
            self.writes[b] = [(sem, post)]
            self.reads[b] = []
        for b in reads:
            rl = [(s, v) for (s, v) in self.reads.get(b, ()) if s != sem]
            rl.append((sem, post))
            self.reads[b] = rl
        return (sem, post)

    def mark_write(self, name, sem):
        for k in list(self._conf(name)):
            if k != name:
                self.writes.pop(k, None)
                self.reads.pop(k, None)
        self.writes[name] = [(sem, self.c(sem))]
        self.reads[name] = []

    def emit(self, block, sems, preamble=()):
        def mk(name):
            def run(eng):
                waited = {}
                if name != "sync":
                    for (s, v) in preamble:
                        eng.wait_ge(sems[s], v)
                        waited[s] = v
                for w, fn, sem, amt in self.ev[name]:
                    for (s, v) in w:
                        if waited.get(s, 0) < v:
                            eng.wait_ge(sems[s], v)
                            waited[s] = v
                    ins = fn(eng)
                    if sem is not None:
                        assert ins is not None, f"event on {name} emitted nothing"
                        ins.then_inc(sems[sem], amt)
            return run

        block.sync(mk("sync"))
        block.gpsimd(mk("gpsimd"))
        block.vector(mk("vector"))
        block.scalar(mk("scalar"))
        block.tensor(mk("tensor"))


def build_program(dbg=False):
    nc = bass.Bass("TRN2", target_bir_lowering=False)

    def din(name, shape, dt=F32):
        return nc.dram_tensor(name, shape, dt, kind="ExternalInput").ap()

    xT = din("xT", [D, S])
    xqT = din("xqT", [D, NQ])
    cosk = din("cosk", [HD, S], BF16)
    sink = din("sink", [HD, S], BF16)
    cosq = din("cosq", [HD, NQ], BF16)
    sinq = din("sinq", [HD, NQ], BF16)
    Wq = din("Wq", [D, D])
    Wk = din("Wk", [D, KVD])
    Wv = din("Wv", [D, KVD])
    Wo = din("Wo", [D, D])
    Wr = din("Wr", [NKV, D])
    br = din("br", [1, NKV])
    Al = din("Al", [D, 128])
    Bml = din("Bml", [128, D])
    E8 = din("E8", [NKV, 128], BF16)
    masks = din("masks", [NSLOT, 8, 128, TW], BF16)
    ones_col = din("ones_col", [128, 1], BF16)
    ones_row = din("ones_row", [1, 128], F32)
    ones_row_bf = din("ones_row_bf", [1, 128], BF16)
    piT = din("piT", [128, 128], BF16)
    ident_bf = din("ident_bf", [128, 128], BF16)
    ident_f = din("ident_f", [128, 8], F32)
    constv = din("constv", [128, 12], F32)

    yqT = nc.dram_tensor("yqT", [D, NQ], F32, kind="ExternalOutput").ap()
    wqc = nc.dram_tensor("wqc", [NH, 128, NCD * 128], BF16, kind="Internal").ap()
    woc = nc.dram_tensor("woc", [NH, 128, NCD * 128], BF16, kind="Internal").ap()
    wkc = nc.dram_tensor("wkc", [NKV, 128, NCD * 128], BF16, kind="Internal").ap()
    wvc = nc.dram_tensor("wvc", [NKV, 128, NCD * 128], BF16, kind="Internal").ap()
    sk = "ExternalOutput" if dbg else "Internal"
    kfs = nc.dram_tensor("kfs", [NKV, 128, S], BF16, kind=sk).ap()
    vts = nc.dram_tensor("vts", [NKV, 128, S], BF16, kind=sk).ap()
    if dbg:
        drec = nc.dram_tensor("drec", [1, TW], BF16, kind="ExternalOutput").ap()
        drb = nc.dram_tensor("drb", [128, TW], BF16, kind="ExternalOutput").ap()
        dstats = nc.dram_tensor("dstats", [2 * NTB, TW], F32, kind="ExternalOutput").ap()
        dq = nc.dram_tensor("dq", [NH, 128, TW], BF16, kind="ExternalOutput").ap()
        dattn = nc.dram_tensor("dattn", [NH, 128, TW], BF16, kind="ExternalOutput").ap()
        dgate = nc.dram_tensor("dgate", [NCD, 128, TW], BF16, kind="ExternalOutput").ap()
        dkm = nc.dram_tensor("dkm", [1, NKV], F32, kind="ExternalOutput").ap()
        dnb = nc.dram_tensor("dnb", [NKV, 128], F32, kind="ExternalOutput").ap()

    sch = Sched()
    A = sch.add

    with ExitStack() as ctx:
        en = ctx.enter_context

        def sb(name, shape, dt):
            return en(nc.sbuf_tensor(name, shape, dt))

        XB0 = sb("XB0", [128, NCD * TW], BF16)
        XB1 = sb("XB1", [128, NCD * TW], BF16)

        WKV = [sb(f"wkv{i}", [128, S], BF16) for i in range(4)]
        wk_sb = [WKV[0], WKV[2]]
        wv_sb = [WKV[1], WKV[3]]
        wkn = ["wkv0", "wkv2"]
        wvn = ["wkv1", "wkv3"]
        ksb = [WKV[0], WKV[2]]
        vtb = [WKV[1], WKV[3]]
        ksn = ["wkv0", "wkv2"]
        vtn = ["wkv1", "wkv3"]

        wq_sb = [sb(f"wq{i}", [128, NCD * 128], BF16) for i in range(3)]
        wo_sb = [sb(f"wo{i}", [128, NCD * 128], BF16) for i in range(3)]

        cosb = [sb(f"cosb{i}", [128, TW], BF16) for i in range(2)]
        sinb = [sb(f"sinb{i}", [128, TW], BF16) for i in range(2)]

        tmpA = sb("tmpA", [128, TW], F32)
        tmpB = sb("tmpB", [128, TW], F32)

        qpb = [sb(f"qpb{i}", [128, 4 * TW], BF16) for i in range(2)]
        PP = [sb(f"PP{i}", [128, 2 * TW], BF16) for i in range(2)]
        kraw = [PP[0][:, 0:TW], PP[0][:, TW:2 * TW]]
        krawn = ["PP0.a", "PP0.b"]
        sqh = PP[1][:, 0:TW]
        kout = [qpb[0][:, 0:TW], qpb[0][:, TW:2 * TW]]
        koutn = ["qpb0.h0", "qpb0.h1"]
        vout = [qpb[0][:, 2 * TW:3 * TW], qpb[0][:, 3 * TW:4 * TW]]
        voutn = ["qpb0.h2", "qpb0.h3"]
        vtst = [qpb[1][:, 0:TW], qpb[1][:, TW:2 * TW]]
        vtstn = ["qpb1.h0", "qpb1.h1"]

        m_sb = sb("m_sb", [128, TW], BF16)
        r_sb = sb("r_sb", [128, TW], BF16)
        rb0, rb1 = m_sb, r_sb
        xf32 = [sb(f"xf32{i}", [128, TW], F32) for i in range(2)]
        tmpc = tmpA

        mask_sb = sb("mask_sb", [128, 8 * TW], BF16)
        low_sb = sb("low_sb", [128, TW], BF16)
        scb_sb = sb("scb_sb", [128, TW], BF16)
        stsq = [low_sb, scb_sb]
        stsqn = ["low_sb", "scb_sb"]
        lg_sb = tmpB[0:8, 0:TW]
        lgt_sb = sb("lgt_sb", [128, 32], F32)
        et_sb = sb("et_sb", [128, 32], F32)
        rmax4 = sb("rmax4", [128, 4], F32)
        esum4 = sb("esum4", [128, 4], F32)
        rcp4 = sb("rcp4", [128, 4], F32)
        sct_sb = sb("sct_sb", [128, 32], BF16)
        sfm_sb = tmpB[0:8, 0:256].bitcast(BF16)

        srow = sb("srow", [1, TW], F32)
        m_row = sb("m_row", [1, TW], F32)
        r_row = sb("r_row", [1, TW], BF16)
        drr = [sb(f"drr{i}", [1, TW], BF16) for i in range(4)]
        mrbf = drr[0]
        kmax2 = sb("kmax2", [1, NKV], F32)
        qn4 = sb("qn4", [1, 4], F32)
        sc1 = sb("sc1", [1, 1], F32)
        negB = [sb(f"negB{i}", [128, 1], F32) for i in range(2)]

        onec = sb("onec", [128, 1], BF16)
        oner = sb("oner", [1, 128], F32)
        onerb = sb("onerb", [1, 128], BF16)
        pib = sb("pib", [128, 128], BF16)
        idb = sb("idb", [128, 128], BF16)
        idf = sb("idf", [128, 8], F32)
        e8b = sb("e8b", [8, 128], BF16)
        brb = sb("brb", [1, NKV], F32)
        br_b = sb("br_b", [128, NKV], F32)
        wr_bf = XB0[0:8, 0:D]
        wrt = sb("wrt", [128, NCD * 8], BF16)
        a_sb = sb("a_sb", [128, NCD * 128], BF16)
        bm_sb = sb("bm_sb", [128, D], BF16)

        cvt = sb("cvt", [128, 12], F32)
        pbigA = en(nc.psum_tensor("pbigA", [128, 1024], F32))
        pbigB = en(nc.psum_tensor("pbigB", [128, 1024], F32))
        A0 = en(nc.psum_tensor("A0", [128, 512], F32))
        A1 = en(nc.psum_tensor("A1", [128, 512], F32))
        D0 = en(nc.psum_tensor("D0", [128, 512], F32))
        D1 = en(nc.psum_tensor("D1", [128, 512], F32))
        d0bf = D0[:].bitcast(BF16)
        a1bf = A1[:].bitcast(BF16)

        sems = {n: en(nc.semaphore(n)) for n in ALL_SEMS}

        blk = en(nc.Block())

        cvt_done = A("sync", lambda s: s.dma_start(out=cvt[:], in_=constv),
                     writes=["cvt"], dma_sem="s_ld")
        for i, val in enumerate(CONST_VALS):
            nc.const_aps.register(F32, val, cvt[:, i:i + 1])
        for dst, nm, src in ((onec, "onec", ones_col), (oner, "oner", ones_row),
                             (onerb, "onerb", ones_row_bf),
                             (pib, "pib", piT), (idb, "idb", ident_bf),
                             (idf, "idf", ident_f), (e8b, "e8b", E8),
                             (brb, "brb", br)):
            A("sync", lambda s, dst=dst, src=src: s.dma_start(out=dst[:], in_=src),
              writes=[nm], dma_sem="s_ld")
        A("gpsimd", lambda g: g.dma_start(
            out=a_sb[:].rearrange("p (c r) -> p c r", c=NCD),
            in_=Al.rearrange("(c p) r -> p c r", p=128)),
          writes=["a_sb"], dma_sem="s_cast")
        A("gpsimd", lambda g: g.dma_start(out=bm_sb[:], in_=Bml),
          writes=["bm_sb"], dma_sem="s_cast")

        A("tensor", lambda t: t.matmul(A0[:, 0:8], oner[:], brb[:], start=True, stop=True),
          reads=["oner", "brb"], writes=["A0"])
        A("vector", lambda v: v.tensor_copy(br_b[:], A0[:, 0:8]),
          reads=["A0"], writes=["br_b"])

        A("gpsimd", lambda g: g.dma_start(out=wr_bf, in_=Wr),
          writes=["XB0"], dma_sem="s_cast")
        for c in range(NCD):
            A("tensor", lambda t, c=c: t.transpose(
                d0bf[:, c * 8:(c + 1) * 8], wr_bf[:, c * 128:(c + 1) * 128],
                idb[0:8, 0:8]),
              reads=["XB0", "idb"], writes=["D0"] if c == 0 else [])
        sch.mark_write("D0", "s_pe")
        A("vector", lambda v: v.tensor_copy(wrt[:], d0bf[:, 0:NCD * 8]),
          reads=["D0"], writes=["wrt"])

        CNAME = {id(wqc): "wqc", id(woc): "woc", id(wkc): "wkc", id(wvc): "wvc"}

        def cache_jobs_list():
            stage = [(wq_sb[i], f"wq{i}") for i in range(3)] + \
                    [(wo_sb[i], f"wo{i}") for i in range(3)]
            jobs = ([(Wq, wqc, n) for n in range(NH)]
                    + [(Wo, woc, n) for n in range(NH)])
            out = []
            for i, (W, C, n) in enumerate(jobs):
                if i < 54:
                    buf, bname = stage[i % len(stage)]
                else:
                    buf, bname = stage[3 + (i - 53) % 3]

                def job(W=W, C=C, n=n, buf=buf, bname=bname):
                    A("gpsimd", lambda g: g.dma_start(
                        out=buf[:].rearrange("p (c n) -> p c n", c=NCD),
                        in_=W[:, n * 128:(n + 1) * 128].rearrange(
                            "(c p) n -> p c n", p=128)),
                      writes=[bname], dma_sem="s_cast")
                    A("sync", lambda s: s.dma_start(out=C[n], in_=buf[:]),
                      reads=[bname], writes=[f"{CNAME[id(C)]}.{n}"],
                      dma_sem="s_st")
                out.append(job)
            return out

        def load_xblock(buf, bufname, src, col0):
            return A("gpsimd", lambda g: g.dma_start(
                out=buf[:].rearrange("p (c t) -> p c t", c=NCD),
                in_=src[:, col0:col0 + TW].rearrange("(c p) t -> p c t", p=128)),
                writes=[bufname], dma_sem="s_cast")

        def load_wcol(W, n0, buf, bufname):
            return A("gpsimd", lambda g: g.dma_start(
                out=buf[:].rearrange("p (c n) -> p c n", c=NCD),
                in_=W[:, n0:n0 + 128].rearrange("(c p) n -> p c n", p=128)),
                writes=[bufname], dma_sem="s_cast")

        def load_wcached(C, n, buf, bufname, eng="sync"):
            return A(eng, lambda s: s.dma_start(out=buf[:], in_=C[n]),
                     reads=[f"{CNAME[id(C)]}.{n}"], writes=[bufname],
                     dma_sem="s_ld")

        def rep(ap2d, n, width):
            return ap2d.unsqueeze(1).to_broadcast([128, n, width])

        def recip_bf(v, out, in_):
            with nc.allow_low_precision(reason="reciprocal feeds bf16 broadcast"):
                return v.reciprocal(out, in_)

        def ln_stats(xbuf, xname, dbg_tb=None, load=None):
            for c in range(NCD + 2):
                if load is not None and c % 8 == 0 and c < NCD:
                    srcT, col0 = load
                    p = c // 8
                    A("gpsimd", lambda g, p=p, srcT=srcT, col0=col0: g.dma_start(
                        out=xbuf[:, p * 8 * TW:(p + 1) * 8 * TW].rearrange(
                            "p (c t) -> p c t", c=8),
                        in_=srcT[p * 8 * 128:(p + 1) * 8 * 128,
                                 col0:col0 + TW].rearrange(
                            "(c p) t -> p c t", p=128)),
                      writes=[f"{xname}.p{p}"]
                             + [f"{xname}.h{8 * p + i}" for i in range(8)]
                             + [f"{xname}.g{8 * p + i}" for i in range(8)],
                      dma_sem="s_cast")
                if c < NCD:
                    sq, sqn = stsq[c % 2], stsqn[c % 2]
                    A("scalar", lambda s, c=c, sq=sq: s.activation(
                        sq[:], xbuf[:, c * TW:(c + 1) * TW], AF.Square),
                      reads=[f"{xname}.p{c // 8}"], writes=[sqn])
                if c >= 2:
                    c2 = c - 2
                    sq2, sq2n = stsq[c2 % 2], stsqn[c2 % 2]
                    A("tensor", lambda t, c2=c2: t.matmul(
                        D0[0:1, :], onec[:], xbuf[:, c2 * TW:(c2 + 1) * TW],
                        start=(c2 == 0), stop=(c2 == NCD - 1)),
                      reads=[f"{xname}.p{c2 // 8}", "onec"],
                      writes=["D0"] if c2 == 0 else [])
                    A("tensor", lambda t, c2=c2, sq2=sq2: t.matmul(
                        D1[0:1, :], onec[:], sq2[:],
                        start=(c2 == 0), stop=(c2 == NCD - 1)),
                      reads=[sq2n, "onec"], writes=["D1"] if c2 == 0 else [])
            sch.mark_write("D0", "s_pe")
            sch.mark_write("D1", "s_pe")
            A("vector", lambda v: v.tensor_scalar_mul(m_row[:], D0[0:1, :], 1.0 / D),
              reads=["D0"], writes=["m_row"])
            A("vector", lambda v: v.tensor_scalar_mul(srow[:], D1[0:1, :], 1.0 / D),
              reads=["D1"], writes=["srow"])
            A("vector", lambda v: v.tensor_mul(r_row[:], m_row[:], m_row[:]),
              reads=["m_row"], writes=["r_row"])
            A("vector", lambda v: v.tensor_sub(srow[:], srow[:], r_row[:]),
              reads=["srow", "r_row"], writes=["srow"])
            A("scalar", lambda s: s.activation(srow[:], srow[:], AF.Ln, bias=EPS),
              reads=["srow"], writes=["srow"])
            A("scalar", lambda s: s.activation(r_row[:], srow[:], AF.Exp, scale=-0.5),
              reads=["srow"], writes=["r_row"])
            A("vector", lambda v: v.tensor_copy(mrbf[:], m_row[:]),
              reads=["m_row"], writes=["drr0"])
            A("tensor", lambda t: t.matmul(D0[:], onerb[:], mrbf[:], start=True, stop=True),
              reads=["drr0", "onerb"], writes=["D0"])
            A("vector", lambda v: v.tensor_copy(m_sb[:], D0[:]),
              reads=["D0"], writes=["m_sb"])
            A("tensor", lambda t: t.matmul(D1[:], onerb[:], r_row[:], start=True, stop=True),
              reads=["r_row", "onerb"], writes=["D1"])
            A("vector", lambda v: v.tensor_copy(r_sb[:], D1[:]),
              reads=["D1"], writes=["r_sb"])
            if dbg and dbg_tb is not None:
                A("sync", lambda s: s.dma_start(
                    out=dstats[2 * dbg_tb:2 * dbg_tb + 1, :], in_=m_row[:]),
                  reads=["m_row"], dma_sem="s_st")
                A("gpsimd", lambda s: s.dma_start(
                    out=dstats[2 * dbg_tb + 1:2 * dbg_tb + 2, :], in_=r_row[:]),
                  reads=["r_row"], dma_sem="s_st")
            x3 = xbuf[:].rearrange("p (c t) -> p c t", c=NCD)
            A("vector", lambda v: v.tensor_tensor(
                x3, x3, rep(m_sb[:], NCD, TW), ALU.subtract),
              reads=[xname, "m_sb"], writes=[xname])
            A("vector", lambda v: v.tensor_tensor(
                x3, x3, rep(r_sb[:], NCD, TW), ALU.mult),
              reads=[xname, "r_sb"], writes=[xname])

        def p1_load(tbn):
            buf, name = (XB0, "XB0") if tbn % 2 == 0 else (XB1, "XB1")
            np_ = tbn % 2
            load_xblock(buf, name, xT, tbn * TW)
            A("sync", lambda s, tbn=tbn, np_=np_: s.dma_start(
                out=cosb[np_][:], in_=cosk[:, tbn * TW:(tbn + 1) * TW]),
              writes=[f"cosb{np_}"], dma_sem="s_ld")
            A("sync", lambda s, tbn=tbn, np_=np_: s.dma_start(
                out=sinb[np_][:], in_=sink[:, tbn * TW:(tbn + 1) * TW]),
              writes=[f"sinb{np_}"], dma_sem="s_ld")

        def p1_stats(tbn):
            buf, name = (XB0, "XB0") if tbn % 2 == 0 else (XB1, "XB1")
            ln_stats(buf, name, dbg_tb=tbn)

        def p2_slot_setup(slot, load_kv=True):
            spar = slot % 2
            xnb_, xnn_ = (XB0, "XB0") if spar == 0 else (XB1, "XB1")
            q0 = slot * TW
            T = TRIPS[slot]
            A("sync", lambda s, q0=q0, spar=spar: s.dma_start(
                out=cosb[spar][:], in_=cosq[:, q0:q0 + TW]),
              writes=[f"cosb{spar}"], dma_sem="s_ld")
            A("sync", lambda s, q0=q0, spar=spar: s.dma_start(
                out=sinb[spar][:], in_=sinq[:, q0:q0 + TW]),
              writes=[f"sinb{spar}"], dma_sem="s_ld")
            A("sync", lambda s, slot=slot: s.dma_start(
                out=mask_sb[:].rearrange("p (j t) -> p j t", j=8),
                in_=masks[slot].rearrange("j p t -> p j t")),
              writes=["mask_sb"], dma_sem="s_ld")
            load_xblock(xnb_, xnn_, xqT, q0)
            ln_stats(xnb_, xnn_)
            if load_kv:
                p2_kv_load(slot)

        def p2_kv_load(slot):
            T = TRIPS[slot]
            A("sync", lambda s, T=T: s.dma_start(
                out=ksb[0][:, 0:T * 128], in_=kfs[0][:, 0:T * 128]),
              reads=["kfs.g0"], writes=[ksn[0]], dma_sem="s_ld")
            A("sync", lambda s, T=T: s.dma_start(
                out=vtb[0][:, 0:T * 128], in_=vts[0][:, 0:T * 128]),
              reads=["vts.g0"], writes=[vtn[0]], dma_sem="s_ld")

        cache_jobs = cache_jobs_list()
        pending_vt = []
        p1_load(0)
        p1_stats(0)
        for g2 in range(2):
            load_wcol(Wk, g2 * 128, wk_sb[g2], wkn[g2])
            load_wcol(Wv, g2 * 128, wv_sb[g2], wvn[g2])

        for tb in range(NTB):
            par = tb % 2
            xbuf, xname = (XB0, "XB0") if par == 0 else (XB1, "XB1")
            t0 = tb * TW

            for g in range(NKV):
                if pending_vt:
                    pending_vt.pop(0)()
                if g == 0 and tb + 1 < NTB:
                    p1_load(tb + 1)
                if g == 5 and tb + 1 < NTB:
                    p1_stats(tb + 1)
                if g == 2 and tb + 1 == NTB:
                    p2_slot_setup(0, load_kv=False)
                    for h3 in range(3):
                        load_wcol(Wq, h3 * 128, wq_sb[h3], f"wq{h3}")
                gp = g % 2
                PK = pbigA if gp == 0 else pbigB
                PKn = "pbigA" if gp == 0 else "pbigB"
                PV_ = pbigB if gp == 0 else pbigA
                PVn = "pbigB" if gp == 0 else "pbigA"
                for c in range(NCD):
                    A("tensor", lambda t, c=c, gp=gp, xbuf=xbuf, PK=PK: t.matmul(
                        PK[:, 0:512], wk_sb[gp][:, c * 128:(c + 1) * 128],
                        xbuf[:, c * TW:(c + 1) * TW],
                        start=(c == 0), stop=(c == NCD - 1)),
                      reads=[wkn[gp], xname], writes=[PKn + ".lo"] if c == 0 else [])
                sch.mark_write(PKn + ".lo", "s_pe")
                A("vector", lambda v, gp=gp, PK=PK: v.tensor_copy(kraw[gp], PK[:, 0:512]),
                  reads=[PKn + ".lo"], writes=[krawn[gp]])
                A("tensor", lambda t, gp=gp, PK=PK: t.matmul(
                    PK[:, 512:1024], pib[:], kraw[gp], start=True, stop=True),
                  reads=[krawn[gp], "pib"], writes=[PKn + ".hi"])
                A("vector", lambda v, gp=gp, par=par: v.tensor_mul(
                    tmpA[:], kraw[gp], cosb[par][:]),
                  reads=[krawn[gp], f"cosb{par}"], writes=["tmpA"])
                A("vector", lambda v, par=par, PK=PK: v.tensor_mul(
                    tmpB[:], PK[:, 512:1024], sinb[par][:]),
                  reads=[PKn + ".hi", f"sinb{par}"], writes=["tmpB"])
                A("vector", lambda v, gp=gp: v.tensor_add(kout[gp], tmpA[:], tmpB[:]),
                  reads=["tmpA", "tmpB"], writes=[koutn[gp]])
                A("gpsimd", lambda gg, gp=gp: gg.tensor_mul(sqh, kout[gp], kout[gp]),
                  reads=[koutn[gp]], writes=["PP1.a"])
                A("tensor", lambda t: t.matmul(A0[0:1, :], onec[:], sqh,
                                               start=True, stop=True),
                  reads=["PP1.a", "onec"], writes=["A0"])
                if tb == 0:
                    A("vector", lambda v, g=g: v.reduce_max(
                        kmax2[0:1, g:g + 1], A0[0:1, :], axis=AX.X),
                      reads=["A0"], writes=["kmax2"])
                else:
                    A("vector", lambda v: v.reduce_max(sc1[:], A0[0:1, :], axis=AX.X),
                      reads=["A0"], writes=["sc1"])
                    A("vector", lambda v, g=g: v.tensor_max(
                        kmax2[0:1, g:g + 1], kmax2[0:1, g:g + 1], sc1[:]),
                      reads=["kmax2", "sc1"], writes=["kmax2"])
                A("sync", lambda s, g=g, t0=t0, gp=gp: s.dma_start(
                    out=kfs[g][:, t0:t0 + TW], in_=kout[gp]),
                  reads=[koutn[gp]], writes=[f"kfs.g{g}"], dma_sem="s_st")

                for c in range(NCD):
                    A("tensor", lambda t, c=c, gp=gp, xbuf=xbuf, PV_=PV_: t.matmul(
                        PV_[:, 0:512], wv_sb[gp][:, c * 128:(c + 1) * 128],
                        xbuf[:, c * TW:(c + 1) * TW],
                        start=(c == 0), stop=(c == NCD - 1)),
                      reads=[wvn[gp], xname], writes=[PVn + ".lo"] if c == 0 else [])
                sch.mark_write(PVn + ".lo", "s_pe")
                A("vector", lambda v, gp=gp, PV_=PV_: v.tensor_copy(vout[gp], PV_[:, 0:512]),
                  reads=[PVn + ".lo"], writes=[voutn[gp]])
                ng = g + 2
                if ng < NKV or tb + 1 < NTB:
                    ng = ng % NKV
                    load_wcol(Wk, ng * 128, wk_sb[gp], wkn[gp])
                    load_wcol(Wv, ng * 128, wv_sb[gp], wvn[gp])
                def _vtr(g=g, t0=t0, gp=gp):
                    for j in range(4):
                        A("tensor", lambda t, j=j, gp=gp: t.transpose(
                            a1bf[:, j * 128:(j + 1) * 128],
                            vout[gp][:, j * 128:(j + 1) * 128], idb[:]),
                          reads=[voutn[gp], "idb"], writes=["A1"] if j == 0 else [])
                    sch.mark_write("A1", "s_pe")
                    A("vector", lambda v, gp=gp: v.tensor_copy(vtst[gp], a1bf[:, 0:512]),
                      reads=["A1"], writes=[vtstn[gp]])
                    A("sync", lambda s, g=g, t0=t0, gp=gp: s.dma_start(
                        out=vts[g][:, t0:t0 + TW], in_=vtst[gp]),
                      reads=[vtstn[gp]], writes=[f"vts.g{g}"], dma_sem="s_st")
                pending_vt.append(_vtr)
                if cache_jobs:
                    cache_jobs.pop(0)()

        assert not cache_jobs
        for fn_ in pending_vt:
            fn_()
        pending_vt.clear()
        if dbg:
            A("sync", lambda s: s.dma_start(out=dkm, in_=kmax2[:]),
              reads=["kmax2"], dma_sem="s_st")
        p2_kv_load(0)

        for slot in range(NSLOT):
            T = TRIPS[slot]
            spar = slot % 2
            xnb, xnname = (XB0, "XB0") if spar == 0 else (XB1, "XB1")
            aob, aoname = (XB1, "XB1") if spar == 0 else (XB0, "XB0")
            q0 = slot * TW

            pending_norm = []

            for g in range(NKV):
                gp = g % 2
                qbuf, qbname = qpb[gp], f"qpb{gp}"
                nbB, nbname = negB[gp], f"negB{gp}"

                projb = ((A0, "A0"), (A1, "A1"), (A0, "A0"), (A1, "A1"))
                rotb = ((D0, "D0"), (D1, "D1"), (D0, "D0"), (D1, "D1"))
                nrmb = ((pbigA[0:1, 0:512], "pbigA.lo"),
                        (pbigA[0:1, 512:1024], "pbigA.hi"),
                        (pbigB[0:1, 0:512], "pbigB.lo"),
                        (pbigB[0:1, 512:1024], "pbigB.hi"))
                krs = (PP[0][:, 0:TW], PP[0][:, TW:2 * TW],
                       PP[1][:, 0:TW], PP[1][:, TW:2 * TW])
                krn = ("PP0.a", "PP0.b", "PP1.a", "PP1.b")

                def proj(hl, g=g, xnb=xnb, xnname=xnname, projb=projb,
                         krs=krs, krn=krn, slot=slot):
                    h = 4 * g + hl
                    QB, QBn = projb[hl]
                    wqi = (slot * NH + h) % 3
                    for c in range(NCD):
                        A("tensor", lambda t, c=c, QB=QB, xnb=xnb, wqi=wqi: t.matmul(
                            QB[:], wq_sb[wqi][:, c * 128:(c + 1) * 128],
                            xnb[:, c * TW:(c + 1) * TW],
                            start=(c == 0), stop=(c == NCD - 1)),
                          reads=[f"wq{wqi}", xnname], writes=[QBn] if c == 0 else [])
                    sch.mark_write(QBn, "s_pe")
                    A("vector", lambda v, hl=hl, QB=QB, krs=krs: v.tensor_copy(
                        krs[hl], QB[:]),
                      reads=[QBn], writes=[krn[hl]])
                    nh = h + 3
                    if nh < NH or slot + 1 < NSLOT:
                        wqi2 = (slot * NH + nh) % 3
                        load_wcached(wqc, nh % NH, wq_sb[wqi2], f"wq{wqi2}")

                def rot(hl, rotb=rotb, krs=krs, krn=krn, qbuf=qbuf,
                        qbname=qbname, spar=spar):
                    RB, RBn = rotb[hl]
                    A("tensor", lambda t, RB=RB, krs=krs, hl=hl: t.matmul(
                        RB[:], pib[:], krs[hl], start=True, stop=True),
                      reads=[krn[hl], "pib"], writes=[RBn])
                    A("vector", lambda v, hl=hl, krs=krs, spar=spar: v.tensor_mul(
                        tmpA[:], krs[hl], cosb[spar][:]),
                      reads=[krn[hl], f"cosb{spar}"], writes=["tmpA"])
                    A("vector", lambda v, RB=RB, spar=spar: v.tensor_mul(
                        tmpB[:], RB[:], sinb[spar][:]),
                      reads=[RBn, f"sinb{spar}"], writes=["tmpB"])
                    A("vector", lambda v, hl=hl, qbuf=qbuf: v.tensor_add(
                        qbuf[:, hl * TW:(hl + 1) * TW], tmpA[:], tmpB[:]),
                      reads=["tmpA", "tmpB"], writes=[f"{qbname}.h{hl}"])
                    A("gpsimd", lambda gg, hl=hl, qbuf=qbuf, krs=krs: gg.tensor_mul(
                        krs[hl], qbuf[:, hl * TW:(hl + 1) * TW],
                        qbuf[:, hl * TW:(hl + 1) * TW]),
                      reads=[f"{qbname}.h{hl}"], writes=[krn[hl]])

                def nrm(hl, nrmb=nrmb, krs=krs, krn=krn):
                    NB, NBn = nrmb[hl]
                    A("tensor", lambda t, NB=NB, hl=hl, krs=krs: t.matmul(
                        NB, onec[:], krs[hl], start=True, stop=True),
                      reads=[krn[hl], "onec"], writes=[NBn])
                    A("vector", lambda v, hl=hl, NB=NB: v.reduce_max(
                        qn4[0:1, hl:hl + 1], NB, axis=AX.X),
                      reads=[NBn], writes=["qn4"])

                proj(0); proj(1)
                for fn_ in pending_norm:
                    fn_()
                pending_norm.clear()
                rot(0); proj(2); rot(1); proj(3)
                rot(2); rot(3)
                nrm(0); nrm(1); nrm(2); nrm(3)
                A("vector", lambda v: v.reduce_max(sc1[:], qn4[:], axis=AX.X),
                  reads=["qn4"], writes=["sc1"])
                A("vector", lambda v, g=g: v.tensor_mul(
                    sc1[:], sc1[:], kmax2[0:1, g:g + 1]),
                  reads=["sc1", "kmax2"], writes=["sc1"])
                A("scalar", lambda s: s.activation(sc1[:], sc1[:], AF.Ln),
                  reads=["sc1"], writes=["sc1"])
                A("scalar", lambda s: s.activation(sc1[:], sc1[:], AF.Exp, scale=0.5),
                  reads=["sc1"], writes=["sc1"])
                A("vector", lambda v: v.tensor_scalar_mul(sc1[:], sc1[:], -ISQ),
                  reads=["sc1"], writes=["sc1"])

                def _negb(nbB=nbB, nbname=nbname):
                    A("tensor", lambda t: t.matmul(A0[:, 0:1], oner[:], sc1[:],
                                                   start=True, stop=True),
                      reads=["sc1", "oner"], writes=["A0"])
                    A("vector", lambda v, nbB=nbB: v.tensor_copy(nbB[:], A0[:, 0:1]),
                      reads=["A0"], writes=[nbname])
                negb_fn = _negb
                if dbg and slot == 0:
                    for hl in range(4):
                        A("sync", lambda s, g=g, hl=hl, qbuf=qbuf: s.dma_start(
                            out=dq[4 * g + hl], in_=qbuf[:, hl * TW:(hl + 1) * TW]),
                          reads=[f"{qbname}.h{hl}"], dma_sem="s_st")


                if g + 1 < NKV:
                    ngp = (g + 1) % 2
                    A("sync", lambda s, g=g, T=T, ngp=ngp: s.dma_start(
                        out=ksb[ngp][:, 0:T * 128], in_=kfs[g + 1][:, 0:T * 128]),
                      reads=[f"kfs.g{g + 1}"], writes=[ksn[ngp]], dma_sem="s_ld")
                    A("sync", lambda s, g=g, T=T, ngp=ngp: s.dma_start(
                        out=vtb[ngp][:, 0:T * 128], in_=vts[g + 1][:, 0:T * 128]),
                      reads=[f"vts.g{g + 1}"], writes=[vtn[ngp]], dma_sem="s_ld")

                for eh in range(2):
                    ha = 4 * g + 2 * eh
                    la, lb = 2 * eh, 2 * eh + 1

                    def s_mm(kt, gp=gp, qbuf=qbuf, qbname=qbname, la=la,
                             lb=lb, T=T):
                        kp = kt % 2
                        PB = pbigA if kp == 0 else pbigB
                        PBn = "pbigA" if kp == 0 else "pbigB"
                        msk = kt >= T - 8
                        j = kt - (T - 8)
                        A("tensor", lambda t, kt=kt, PB=PB, gp=gp, qbuf=qbuf, la=la,
                          msk=msk: t.matmul(
                            PB[:, 0:512], ksb[gp][:, kt * 128:(kt + 1) * 128],
                            qbuf[:, la * TW:(la + 1) * TW], start=True, stop=not msk),
                          reads=[ksn[gp], f"{qbname}.h{la}"], writes=[PBn + ".lo"])
                        if msk:
                            A("tensor", lambda t, j=j, PB=PB: t.matmul(
                                PB[:, 0:512], idb[:],
                                mask_sb[:, j * TW:(j + 1) * TW],
                                start=False, stop=True),
                              reads=["idb", "mask_sb"], writes=[])
                            sch.mark_write(PBn + ".lo", "s_pe")
                        A("tensor", lambda t, kt=kt, PB=PB, gp=gp, qbuf=qbuf, lb=lb,
                          msk=msk: t.matmul(
                            PB[:, 512:1024], ksb[gp][:, kt * 128:(kt + 1) * 128],
                            qbuf[:, lb * TW:(lb + 1) * TW], start=True, stop=not msk),
                          reads=[ksn[gp], f"{qbname}.h{lb}"], writes=[PBn + ".hi"])
                        if msk:
                            A("tensor", lambda t, j=j, PB=PB: t.matmul(
                                PB[:, 512:1024], idb[:],
                                mask_sb[:, j * TW:(j + 1) * TW],
                                start=False, stop=True),
                              reads=["idb", "mask_sb"], writes=[])
                            sch.mark_write(PBn + ".hi", "s_pe")

                    s_mm(0)
                    if eh == 0 and negb_fn is not None:
                        negb_fn()
                        negb_fn = None
                    for kt in range(T):
                        kp = kt % 2
                        PB = pbigA if kp == 0 else pbigB
                        PBn = "pbigA" if kp == 0 else "pbigB"
                        Pb, Pbn = PP[kp], f"PP{kp}"
                        if kt + 1 < T:
                            s_mm(kt + 1)
                        A("scalar", lambda s, PB=PB, Pb=Pb, nbB=nbB: s.activation(
                            Pb[:], PB[:], AF.Exp, bias=nbB[:, 0:1], scale=ISQ),
                          reads=[PBn + ".lo", PBn + ".hi", nbname], writes=[Pbn])
                        A("tensor", lambda t, kt=kt, Pb=Pb, gp=gp, T=T: t.matmul(
                            A0[:], vtb[gp][:, kt * 128:(kt + 1) * 128], Pb[:, 0:512],
                            start=(kt == 0), stop=(kt == T - 1)),
                          reads=[vtn[gp], Pbn + ".lo"], writes=["A0"] if kt == 0 else [])
                        A("tensor", lambda t, kt=kt, Pb=Pb, T=T: t.matmul(
                            D0[0:1, :], onec[:], Pb[:, 0:512],
                            start=(kt == 0), stop=(kt == T - 1)),
                          reads=["onec", Pbn + ".lo"], writes=["D0"] if kt == 0 else [])
                        A("tensor", lambda t, kt=kt, Pb=Pb, gp=gp, T=T: t.matmul(
                            A1[:], vtb[gp][:, kt * 128:(kt + 1) * 128], Pb[:, 512:1024],
                            start=(kt == 0), stop=(kt == T - 1)),
                          reads=[vtn[gp], Pbn + ".hi"], writes=["A1"] if kt == 0 else [])
                        A("tensor", lambda t, kt=kt, Pb=Pb, T=T: t.matmul(
                            D1[0:1, :], onec[:], Pb[:, 512:1024],
                            start=(kt == 0), stop=(kt == T - 1)),
                          reads=["onec", Pbn + ".hi"], writes=["D1"] if kt == 0 else [])
                    for nmx in ("A0", "A1", "D0", "D1"):
                        sch.mark_write(nmx, "s_pe")
                    A("vector", lambda v, ha=ha, aob=aob: v.tensor_copy(
                        aob[:, ha * TW:(ha + 1) * TW], A0[:]),
                      reads=["A0"], writes=[f"{aoname}.h{ha}"])
                    A("vector", lambda v, ha=ha, aob=aob: v.tensor_copy(
                        aob[:, (ha + 1) * TW:(ha + 2) * TW], A1[:]),
                      reads=["A1"], writes=[f"{aoname}.h{ha + 1}"])
                    A("vector", lambda v, eh=eh: v.tensor_copy(
                        drr[2 * eh][:], D0[0:1, :]),
                      reads=["D0"], writes=[f"drr{2 * eh}"])
                    A("vector", lambda v, eh=eh: v.tensor_copy(
                        drr[2 * eh + 1][:], D1[0:1, :]),
                      reads=["D1"], writes=[f"drr{2 * eh + 1}"])

                    def _norm(ha=ha, eh=eh, aob=aob, aoname=aoname, g=g,
                              slot=slot):
                        for i2 in range(2):
                            dri, drn = drr[2 * eh + i2], f"drr{2 * eh + i2}"
                            half = slice(0, 512) if i2 == 0 else slice(512, 1024)
                            hname = f"{aoname}.h{ha + i2}"
                            A("scalar", lambda s, dri=dri: s.activation(
                                srow[:], dri[:], AF.Ln),
                              reads=[drn], writes=["srow"])
                            A("scalar", lambda s, dri=dri: s.activation(
                                dri[:], srow[:], AF.Exp, scale=-1.0),
                              reads=["srow"], writes=[drn])
                            A("tensor", lambda t, dri=dri, half=half: t.matmul(
                                pbigA[:, half], onerb[:], dri[:],
                                start=True, stop=True),
                              reads=[drn, "onerb"],
                              writes=["pbigA.lo" if half.start == 0 else "pbigA.hi"])
                            rb = rb0 if i2 == 0 else rb1
                            rbn = "m_sb" if i2 == 0 else "r_sb"
                            A("vector", lambda v, rb=rb, half=half: v.tensor_copy(
                                rb[:], pbigA[:, half]),
                              reads=["pbigA.lo" if half.start == 0 else "pbigA.hi"],
                              writes=[rbn])
                            A("vector", lambda v, ha=ha, i2=i2, aob=aob, rb=rb:
                              v.tensor_tensor(
                                aob[:, (ha + i2) * TW:(ha + i2 + 1) * TW],
                                aob[:, (ha + i2) * TW:(ha + i2 + 1) * TW],
                                rb[:], ALU.mult),
                              reads=[hname, rbn], writes=[hname])
                            if dbg and slot == 0:
                                A("sync", lambda s, ha=ha, i2=i2, aob=aob:
                                  s.dma_start(
                                    out=dattn[ha + i2],
                                    in_=aob[:, (ha + i2) * TW:(ha + i2 + 1) * TW]),
                                  reads=[hname], dma_sem="s_st")
                    pending_norm.append(_norm)

            for fn_ in pending_norm:
                fn_()
            pending_norm.clear()
            for c in range(NCD):
                A("tensor", lambda t, c=c, xnb=xnb: t.matmul(
                    A0[0:8, :], wrt[:, c * 8:(c + 1) * 8],
                    xnb[:, c * TW:(c + 1) * TW],
                    start=(c == 0), stop=(c == NCD - 1)),
                  reads=["wrt", xnname], writes=["A0"] if c == 0 else [])
            sch.mark_write("A0", "s_pe")
            A("vector", lambda v: v.tensor_copy(lg_sb, A0[0:8, :]),
              reads=["A0"], writes=["tmpB"])
            for j in range(4):
                A("tensor", lambda t, j=j: t.transpose(
                    A1[:, j * 8:(j + 1) * 8], lg_sb[:, j * 128:(j + 1) * 128],
                    idf[0:8, 0:8]),
                  reads=["tmpB", "idf"], writes=["A1"] if j == 0 else [])
            sch.mark_write("A1", "s_pe")
            A("vector", lambda v: v.tensor_copy(lgt_sb[:], A1[:, 0:32]),
              reads=["A1"], writes=["lgt_sb"])
            A("vector", lambda v: v.tensor_tensor(
                lgt_sb[:].rearrange("p (j h) -> p j h", j=4),
                lgt_sb[:].rearrange("p (j h) -> p j h", j=4),
                br_b[:].unsqueeze(1).to_broadcast([128, 4, NKV]), ALU.add),
              reads=["lgt_sb", "br_b"], writes=["lgt_sb"])
            A("vector", lambda v: v.reduce_max(
                rmax4[:], lgt_sb[:].rearrange("p (j h) -> p j h", j=4),
                axis=AX.X, negate=True),
              reads=["lgt_sb"], writes=["rmax4"])
            for j in range(4):
                A("scalar", lambda s, j=j: s.activation(
                    et_sb[:, j * 8:(j + 1) * 8], lgt_sb[:, j * 8:(j + 1) * 8],
                    AF.Exp, bias=rmax4[:, j:j + 1],
                    accum_out=esum4[:, j:j + 1]),
                  reads=["lgt_sb", "rmax4"], writes=["et_sb", "esum4"])
            A("vector", lambda v: v.reciprocal(rcp4[:], esum4[:]),
              reads=["esum4"], writes=["rcp4"])
            for j in range(4):
                A("vector", lambda v, j=j: v.tensor_scalar(
                    sct_sb[:, j * 8:(j + 1) * 8], et_sb[:, j * 8:(j + 1) * 8],
                    rcp4[:, j:j + 1], float(NKV), ALU.mult, ALU.mult),
                  reads=["et_sb", "rcp4"], writes=["sct_sb"])
            for j in range(4):
                A("tensor", lambda t, j=j: t.transpose(
                    d0bf[0:8, j * 128:(j + 1) * 128], sct_sb[:, j * 8:(j + 1) * 8],
                    idb[:]),
                  reads=["sct_sb", "idb"], writes=["D0"] if j == 0 else [])
            sch.mark_write("D0", "s_pe")
            A("vector", lambda v: v.tensor_copy(sfm_sb, d0bf[0:8, 0:TW]),
              reads=["D0"], writes=["tmpB"])
            A("tensor", lambda t: t.matmul(A0[:], e8b[:], sfm_sb,
                                           start=True, stop=True),
              reads=["e8b", "tmpB"], writes=["A0"])
            A("vector", lambda v: v.tensor_copy(scb_sb[:], A0[:]),
              reads=["A0"], writes=["scb_sb"])
            for c in range(NCD):
                A("tensor", lambda t, c=c, xnb=xnb: t.matmul(
                    A1[:], a_sb[:, c * 128:(c + 1) * 128],
                    xnb[:, c * TW:(c + 1) * TW],
                    start=(c == 0), stop=(c == NCD - 1)),
                  reads=["a_sb", xnname], writes=["A1"] if c == 0 else [])
            sch.mark_write("A1", "s_pe")
            A("vector", lambda v: v.tensor_tensor(low_sb[:], A1[:], scb_sb[:], ALU.mult),
              reads=["A1", "scb_sb"], writes=["low_sb"])
            for oc in range(NCD):
                PB = D0 if oc % 2 == 0 else D1
                PBn = "D0" if oc % 2 == 0 else "D1"
                A("tensor", lambda t, oc=oc, PB=PB: t.matmul(
                    PB[:], bm_sb[:, oc * 128:(oc + 1) * 128], low_sb[:],
                    start=True, stop=True),
                  reads=["bm_sb", "low_sb"], writes=[PBn])
                A("scalar", lambda s, oc=oc, PB=PB, xnb=xnb: s.activation(
                    xnb[:, oc * TW:(oc + 1) * TW], PB[:], AF.Silu, scale=2.0),
                  reads=[PBn], writes=[f"{xnname}.g{oc}"])
                if dbg and slot == 0:
                    A("sync", lambda s, oc=oc, xnb=xnb: s.dma_start(
                        out=dgate[oc], in_=xnb[:, oc * TW:(oc + 1) * TW]),
                      reads=[f"{xnname}.g{oc}"], dma_sem="s_st")

            for i3 in range(3):
                load_wcached(woc, i3, wo_sb[i3], f"wo{i3}", eng="gpsimd")
            A("sync", lambda s, q0=q0: s.dma_start(
                out=xf32[0][:], in_=xqT[0:128, q0:q0 + TW]),
              writes=["xf320"], dma_sem="s_ld")
            for nck in range(NCD):
                wp = nck % 2
                wb3 = nck % 3
                PB = pbigA if wp == 0 else pbigB
                PBn = "pbigA" if wp == 0 else "pbigB"
                for c in range(NCD):
                    A("tensor", lambda t, c=c, PB=PB, wb3=wb3, aob=aob: t.matmul(
                        PB[:, 0:512], wo_sb[wb3][:, c * 128:(c + 1) * 128],
                        aob[:, c * TW:(c + 1) * TW],
                        start=(c == 0), stop=(c == NCD - 1)),
                      reads=[f"wo{wb3}", f"{aoname}.h{c}"],
                      writes=[PBn + ".lo"] if c == 0 else [])
                sch.mark_write(PBn + ".lo", "s_pe")
                if nck + 3 < NCD:
                    load_wcached(woc, nck + 3, wo_sb[(nck + 3) % 3],
                                 f"wo{(nck + 3) % 3}", eng="gpsimd")
                if nck + 1 < NCD:
                    A("sync", lambda s, nck=nck, q0=q0: s.dma_start(
                        out=xf32[(nck + 1) % 2][:],
                        in_=xqT[(nck + 1) * 128:(nck + 2) * 128, q0:q0 + TW]),
                      writes=[f"xf32{(nck + 1) % 2}"], dma_sem="s_ld")
                A("vector", lambda v, nck=nck, PB=PB, xnb=xnb: v.tensor_tensor(
                    tmpc[:], PB[:, 0:512], xnb[:, nck * TW:(nck + 1) * TW], ALU.mult),
                  reads=[PBn + ".lo", f"{xnname}.g{nck}"], writes=["tmpA"])
                A("vector", lambda v, wp=wp: v.tensor_add(
                    xf32[wp][:], tmpc[:], xf32[wp][:]),
                  reads=["tmpA"], writes=[f"xf32{wp}"])
                A("sync", lambda s, nck=nck, q0=q0, wp=wp: s.dma_start(
                    out=yqT[nck * 128:(nck + 1) * 128, q0:q0 + TW], in_=xf32[wp][:]),
                  reads=[f"xf32{wp}"], dma_sem="s_st")
            if slot + 1 < NSLOT:
                p2_slot_setup(slot + 1)

        A("sync", lambda s: s.nop(),
          extra_waits=[(ln, sch.c(ln))
                       for ln in DMA_LANES["sync"] + DMA_LANES["gpsimd"]],
          no_inc=True)

        sch.emit(blk, sems, preamble=(cvt_done,))

    return nc


def _cos_sin_tables():
    inv = 1.0 / (10000.0 ** (np.arange(0, HD, 2, dtype=np.float64) / HD))
    pos = np.arange(S, dtype=np.float64)
    fr = pos[:, None] * inv[None, :]
    emb = np.concatenate([fr, fr], axis=1)
    return (np.ascontiguousarray(np.cos(emb).astype(np.float32).T),
            np.ascontiguousarray(np.sin(emb).astype(np.float32).T))


def make_in_maps(x, Wq_, Wk_, Wv_, Wo_, A_, B_, Wr_, br_):
    cosT, sinT = _cos_sin_tables()

    piT = np.zeros((128, 128), dtype=np.float32)
    for a in range(64):
        piT[64 + a, a] = -1.0
        piT[a, 64 + a] = 1.0
    E8m = np.zeros((NKV, 128), dtype=np.float32)
    for h in range(NKV):
        E8m[h, h * 16:(h + 1) * 16] = 1.0

    Al = np.ascontiguousarray(np.asarray(A_, np.float32).reshape(128, D).T)
    Bml = np.ascontiguousarray(
        np.asarray(B_, np.float32).transpose(0, 2, 1).reshape(128, D))

    shared = {
        "cosk": cosT.astype(BF), "sink": sinT.astype(BF),
        "Wq": np.ascontiguousarray(np.asarray(Wq_, np.float32)),
        "Wk": np.ascontiguousarray(np.asarray(Wk_, np.float32)),
        "Wv": np.ascontiguousarray(np.asarray(Wv_, np.float32)),
        "Wo": np.ascontiguousarray(np.asarray(Wo_, np.float32)),
        "Wr": np.ascontiguousarray(np.asarray(Wr_, np.float32)),
        "br": np.asarray(br_, np.float32).reshape(1, NKV),
        "Al": Al, "Bml": Bml,
        "E8": E8m.astype(BF),
        "ones_col": np.ones((128, 1), np.float32).astype(BF),
        "ones_row": np.ones((1, 128), np.float32),
        "ones_row_bf": np.ones((1, 128), np.float32).astype(BF),
        "piT": piT.astype(BF),
        "ident_bf": np.eye(128, dtype=np.float32).astype(BF),
        "ident_f": np.eye(128, dtype=np.float32)[:, 0:8].copy(),
        "constv": np.tile(np.pad(np.array(CONST_VALS, np.float32), (0, 12 - len(CONST_VALS))), (128, 1)),
    }

    in_maps = []
    for c in range(8):
        b, sub = c // 2, c % 2
        js = J_SETS[sub]
        xb = x[b]
        xTc = np.ascontiguousarray(xb.T)
        qcols = np.concatenate([np.arange(512 * J, 512 * (J + 1)) for J in js])
        xqTc = np.ascontiguousarray(xb[qcols].T)
        cosqc = np.ascontiguousarray(cosT[:, qcols])
        sinqc = np.ascontiguousarray(sinT[:, qcols])
        mk = np.zeros((NSLOT, 8, 128, TW), np.float32)
        for i, J in enumerate(js):
            T = TRIPS[i]
            for j in range(8):
                kt = T - 8 + j
                kglob = 128 * kt + np.arange(128)[:, None]
                qglob = 512 * J + np.arange(TW)[None, :]
                mk[i, j] = np.where(qglob >= kglob, 0.0, -30000.0)
        m = dict(shared)
        m.update({"xT": xTc, "xqT": xqTc, "cosq": cosqc.astype(BF),
                  "sinq": sinqc.astype(BF), "masks": mk.astype(BF)})
        in_maps.append(m)
    return in_maps


def kernel(L0_attn_hidden_states, L0_attn_Wq, L0_attn_Wk, L0_attn_Wv, L0_attn_Wo,
           L0_ffn_A, L0_ffn_B, L0_ffn_Wr, L0_ffn_br):
    x = np.asarray(L0_attn_hidden_states, dtype=np.float32)
    B = x.shape[0]
    in_maps = make_in_maps(x, L0_attn_Wq, L0_attn_Wk, L0_attn_Wv, L0_attn_Wo,
                           L0_ffn_A, L0_ffn_B, L0_ffn_Wr, L0_ffn_br)
    nc = build_program()
    res = run_bass_kernel_spmd(nc, in_maps, core_ids=list(range(8)))

    out = np.empty((B, S, D), dtype=np.float32)
    for c in range(8):
        b, sub = c // 2, c % 2
        js = J_SETS[sub]
        yq = res.results[c]["yqT"].T
        for i, J in enumerate(js):
            out[b, 512 * J:512 * (J + 1), :] = yq[i * TW:(i + 1) * TW, :]
    return out
